# version 33
# baseline (speedup 1.0000x reference)
import numpy as np
import ml_dtypes
import concourse.bass as bass
import concourse.mybir as mybir
from concourse.bass_utils import run_bass_kernel_spmd
from contextlib import ExitStack

F32 = mybir.dt.float32
BF16 = mybir.dt.bfloat16
AF = mybir.ActivationFunctionType
ALU = mybir.AluOpType
AX = mybir.AxisListType

T = 4096
D = 1024
DOWN = 512
NST = 16
TC = 512
TSUB = TC // 128
KF = D // 128
KO = DOWN // 128
NCH = T // TC
NHH = 5
GPC = KO * NST
MAGIC = float(1.5 * 2 ** 23)
MAGIC52 = float(1.5 * 2 ** 52)
MAGDIFF = float(1.5 * 2 ** 52 - 1.5 * 2 ** 23)

(CSUM, CSQ, CM, CMSQ, CMM, CVAR, CVEP, CRVP, CVR, CRSTD, CPR, CMX, CAM,
 CAMC, CREC, CS127, CT1, CQQ, CT2, CBB2, CKFN, CLNT, CLN2,
 CMR) = [4 * i for i in range(24)]


class Em:

    def __init__(self, name, sems, ms, eng=None):
        self.name = name
        self.sems = sems
        self.ms = ms
        self.eng = eng
        self.cnt = {k: 0 for k in sems}
        self.lastw = {}

    def op(self, queue, fn, waits=(), label=None, sem=None, dinc=1, fence=False):
        sem = sem or queue
        if self.name is None:
            for w in waits:
                if isinstance(w, tuple):
                    continue
                assert w.startswith("pcfree_") or w in self.ms, \
                    f"wait on undefined label {w}"
            if label is not None:
                self.cnt[sem] += dinc
                assert label not in self.ms, f"dup label {label}"
                self.ms[label] = (sem, self.cnt[sem])
            return
        if queue != self.name:
            return
        for w in waits:
            force = False
            if isinstance(w, tuple):
                w, m2 = w
                force = (m2 is True)
            sname, val = self.ms[w]
            if sname == self.name and not force:
                continue
            if self.lastw.get(sname, -1) >= val:
                continue
            self.eng.wait_ge(self.sems[sname], val)
            self.lastw[sname] = val
        inst = fn(self.eng)
        if fence:
            inst = self.eng.drain()
        if label is not None:
            sname, _ = self.ms[label]
            inst.then_inc(self.sems[sname], dinc)

    def wait(self, queue, label):
        if self.name is None:
            assert label in self.ms, label
            return
        if queue != self.name:
            return
        sname, val = self.ms[label]
        if self.lastw.get(sname, -1) >= val:
            return
        self.eng.wait_ge(self.sems[sname], val)
        self.lastw[sname] = val


def build_nc(nch=NCH):
    nc = bass.Bass("TRN2", target_bir_lowering=False, debug=False,
                   detect_race_conditions=False)

    Xd = nc.dram_tensor("x", [T, D], F32, kind="ExternalInput").ap()
    Wd = nc.dram_tensor("wt", [128, KF * KO * 128], BF16, kind="ExternalInput").ap()
    XPWd = nc.dram_tensor("xpwt", [128, KF * 96], F32, kind="ExternalInput").ap()
    DTWd = nc.dram_tensor("dtwt", [64, DOWN], F32, kind="ExternalInput").ap()
    ALOGd = nc.dram_tensor("alog", [128, KO * NST], F32, kind="ExternalInput").ap()
    COLSd = nc.dram_tensor("cols", [128, 20], F32, kind="ExternalInput").ap()
    WSd = nc.dram_tensor("wsc", [1, 1], F32, kind="ExternalInput").ap()
    ID32d = nc.dram_tensor("id32", [32, 32], F32, kind="ExternalInput").ap()
    EYEd = nc.dram_tensor("eye", [128, 128], F32, kind="ExternalInput").ap()
    EYEBd = nc.dram_tensor("eyeb", [128, 128], BF16, kind="ExternalInput").ap()
    ONESd = nc.dram_tensor("ones1", [1, 128], F32, kind="ExternalInput").ap()
    FCd = nc.dram_tensor("fconst", [128, 4], F32, kind="ExternalInput").ap()
    CSTGd = nc.dram_tensor("cstg", [2, NST, TC], BF16, kind="Internal").ap()
    Od = nc.dram_tensor("o", [T, DOWN], F32, kind="ExternalOutput").ap()

    with ExitStack() as st:
        def sb(name, shape, dt=F32):
            return st.enter_context(nc.sbuf_tensor(name, shape, dt))

        def ps(name, shape, dt=F32):
            return st.enter_context(nc.psum_tensor(name, shape, dt))

        xs = [sb("xs0", [128, TSUB * D]), sb("xs1", [128, TSUB * D])]
        xn = sb("xn", [128, TSUB * D])
        qt = sb("qt", [128, TSUB * D], BF16)
        xnf = sb("xnf", [128, KF * TC], BF16)
        qf = sb("qf", [128, KF * TC], BF16)
        uo = [sb("uo0", [128, KO * TC]), sb("uo1", [128, KO * TC])]
        dl = [sb("dl0", [128, KO * TC]), sb("dl1", [128, KO * TC])]
        du = [sb("du0", [128, KO * TC]), sb("du1", [128, KO * TC])]
        gg = [sb("gg0", [128, KO * TC]), sb("gg1", [128, KO * TC])]
        BCs = [sb("BCs0", [32, TC]), sb("BCs1", [32, TC])]
        sgb = sb("sgb", [128, KO * TC])
        dpbf = sb("dpbf", [64, TC], BF16)
        da = [sb("da0", [128, TC]), sb("da1", [128, TC]), sb("da2", [128, TC])]
        db = [sb("db0", [128, TC]), sb("db1", [128, TC]), sb("db2", [128, TC])]
        hh = sb("hh", [128, NHH * TC])
        hcD = sb("hcD", [128, 6 * TC], BF16)
        eyed = sb("eyed", [128, KO * 128])
        BCsb = [sb("bcsb0", [32, TC], BF16), sb("bcsb1", [32, TC], BF16)]
        cring = sb("cring", [128, 4 * TC], BF16)
        spa = sb("spa", [128, TC])
        spb = sb("spb", [128, TC])
        wsb = sb("wsb", [128, KF * KO * 128], BF16)
        xpwb = sb("xpwb", [128, KF * 96], BF16)
        dtwb = sb("dtwb", [64, DOWN], BF16)
        alog = sb("alogs", [128, KO * NST])
        acoef = sb("acoefs", [128, KO * NST])
        colsb = sb("colsb", [128, 20])
        stats = sb("stats", [128, 96])
        prec = sb("prec", [128, 24])
        sigtr = sb("sigtrs", [128, KF])
        state = sb("state", [128, KO * NST])
        id32 = sb("id32s", [32, 32])
        eye = sb("eyes", [128, 128])
        eyeb = sb("eyebs", [128, 128], BF16)
        ones1 = sb("ones1s", [1, 128])
        wssb = sb("wssb", [1, 1])
        krsb = sb("krsb", [128, TC])
        fc = sb("fcs", [128, 4])
        nc.const_aps.register(F32, 1e-5, fc[:, 0:1])
        nc.const_aps.register(F32, 1e-6, fc[:, 1:2])
        nc.const_aps.register(F32, -0.5, fc[:, 2:3])

        ptr = [ps("ptr0", [128, TC]), ps("ptr1", [128, TC])]
        pq = ps("pq", [128, 2 * TC], BF16)
        pg = ps("pg", [128, TC])
        pb = [ps("pb0", [128, TC]), ps("pb1", [128, TC]), ps("pb2", [128, TC])]
        pk = ps("pk", [128, TC])

        sems = {}
        for s_ in ("pe", "act", "dve", "pool", "xin0", "xin1", "oq", "wq",
                   "cst", "cq"):
            sems[s_] = st.enter_context(nc.semaphore("sem_" + s_))

        def dls(ch, ko):
            return dl[ch % 2][:, ko * TC:(ko + 1) * TC]

        def dus(ch, ko):
            return du[ch % 2][:, ko * TC:(ko + 1) * TC]

        def uos(ch, ko):
            return uo[ch % 2][:, ko * TC:(ko + 1) * TC]

        def ggs(ch, ko):
            return gg[ch % 2][:, ko * TC:(ko + 1) * TC]

        def sgbs(ko):
            return sgb[:, ko * TC:(ko + 1) * TC]

        def hhs(g):
            ghp = g % NHH
            return hh[:, ghp * TC:(ghp + 1) * TC]

        def emit_hcmul(E, g):
            ch = g // GPC
            r = g - ch * GPC
            kop, n, koh = r // 32, (r % 32) // 2, r % 2
            hho = hhs(g)
            hcd = hcD[:, (g % 6) * TC:(g % 6 + 1) * TC]
            hw = [f"cbc_{ch}_{kop}_{n}"]
            if g >= 6:
                hw += [f"pacc_{g - 6}"]
            E.op("dve", lambda e, hho=hho, hcd=hcd, n=n: e.tensor_tensor(
                out=hcd, in0=hho, in1=cring[:, (n % 4) * TC:(n % 4 + 1) * TC],
                op=ALU.mult),
                waits=hw, label=f"hc_{g}")

        def emit_pacc(E, g):
            r = g - (g // GPC) * GPC
            kop, n, koh = r // 32, (r % 32) // 2, r % 2
            hcd = hcD[:, (g % 6) * TC:(g % 6 + 1) * TC]
            bank = pg if koh == 0 else pk
            E.op("pe", lambda e, hcd=hcd, bank=bank: e.matmul(
                out=bank[:], lhsT=eyeb[:], rhs=hcd,
                start=False, stop=(n == NST - 1)),
                waits=[f"hc_{g}"], label=f"pacc_{g}")

        def emit_act_st(E, g):
            ch = g // GPC
            r = g - ch * GPC
            kop, n, koh = r // 32, (r % 32) // 2, r % 2
            ko = kop * 2 + koh
            hho = hhs(g)
            scol = state[:, ko * NST + n:ko * NST + n + 1]
            stw = [f"scan_{g}", "st0"]
            if (g - ch * GPC) < GPC - 1:
                stw += [f"dbu_{g + 1}"]
            else:
                stw += [f"scanfin_{ch}"]
            E.op("act", lambda e, hho=hho, scol=scol: e.copy(
                out=scol, in_=hho[:, TC - 1:TC]),
                waits=stw, label=f"st_{g}",
                fence=((g - ch * GPC) == GPC - 1))

        def td_pre_units(ch):
            p = ch % 2
            xsc = xs[p]

            def xss(s):
                return xsc[:, s * D:(s + 1) * D]

            def cw4(c):
                return stats[:, c:c + 4]

            def scol4(c, s):
                return stats[:, c + s:c + s + 1]

            def u_acc1(E, s):
                sw_ = [f"xin_{ch}"]
                if s == 0:
                    sw_ += [f"var_{ch - 1}"] if ch > 0 else []
                    sw_ += [f"odma_{ch - 2}"] if ch > 1 else []
                E.op("act", lambda e, s=s: e.activation(
                    out=du[p][:, 0:D], in_=xss(s), func=AF.Copy,
                    accum_out=scol4(CSUM, s)),
                    waits=sw_, label=f"sum_{ch}" if s == 3 else None, fence=(s == 3))

            def u_acc2(E, s):
                E.op("act", lambda e, s=s: e.activation(
                    out=du[p][:, 0:D], in_=xss(s), func=AF.Square,
                    accum_out=scol4(CSQ, s)),
                    label=f"sq_{ch}_{s}", fence=(s == 3))

            def u_cols(E):
                mw = [f"sum_{ch}"] + ([f"prs_{ch - 1}"] if ch > 0 else [])
                E.op("dve", lambda e: e.tensor_scalar(out=cw4(CM), in0=cw4(CSUM),
                     scalar1=1.0 / D, scalar2=None, op0=ALU.mult), waits=mw, fence=True)
                E.op("dve", lambda e: e.tensor_scalar(out=cw4(CMSQ), in0=cw4(CSQ),
                     scalar1=1.0 / D, scalar2=None, op0=ALU.mult),
                     waits=[f"sq_{ch}_3"], fence=True)
                E.op("dve", lambda e: e.tensor_tensor(out=cw4(CMM), in0=cw4(CM),
                     in1=cw4(CM), op=ALU.mult), fence=True)
                E.op("dve", lambda e: e.tensor_tensor(out=cw4(CVAR), in0=cw4(CMSQ),
                     in1=cw4(CMM), op=ALU.subtract), fence=True)
                E.op("dve", lambda e: e.tensor_scalar(out=cw4(CVEP), in0=cw4(CVAR),
                     scalar1=1e-5, scalar2=None, op0=ALU.add), fence=True)
                E.op("dve", lambda e: e.reciprocal(out=cw4(CRVP), in_=cw4(CVEP)), fence=True)
                E.op("dve", lambda e: e.tensor_tensor(out=cw4(CVR), in0=cw4(CVAR),
                     in1=cw4(CRVP), op=ALU.mult), label=f"var_{ch}", fence=True)
                E.op("act", lambda e: e.activation(out=cw4(CLNT), in_=cw4(CVAR),
                     func=AF.Ln, bias=1e-5),
                     waits=[f"var_{ch}", "ld_fc"] + ([f"q_{ch - 1}_3"] if ch > 0 else []),
                     fence=True)
                E.op("act", lambda e: e.activation(out=cw4(CRSTD), in_=cw4(CLNT),
                     func=AF.Exp, scale=-0.5), fence=True)
                E.op("act", lambda e: e.activation(out=cw4(CLN2), in_=cw4(CVR),
                     func=AF.Ln, bias=1e-6), fence=True)
                E.op("act", lambda e: e.activation(out=cw4(CPR), in_=cw4(CLN2),
                     func=AF.Exp, scale=-0.5), label=f"prs_{ch}", fence=True)

            def u_xn(E):
                E.op("dve", lambda e: e.scalar_tensor_tensor(
                    out=cw4(CMR), in0=cw4(CM), scalar=-1.0, in1=cw4(CRSTD),
                    op0=ALU.mult, op1=ALU.mult),
                    waits=[f"prs_{ch}"], label=f"cmr_{ch}", fence=True)
                for s in range(TSUB):
                    xw = [f"cmr_{ch}"]
                    if ch > 0 and s == 0:
                        xw += [f"xnT_{ch - 1}_{KF - 1}", f"spadd_{ch - 1}_{KO - 1}"]
                    E.op("act", lambda e, s=s: e.activation(
                        out=xn[:, s * D:(s + 1) * D], in_=xss(s), func=AF.Identity,
                        scale=scol4(CRSTD, s), bias=scol4(CMR, s)),
                        waits=xw, label=f"xn_{ch}_{s}")
                    E.op("dve", lambda e, s=s: e.tensor_reduce(
                        out=scol4(CMX, s), in_=xn[:, s * D:(s + 1) * D],
                        axis=AX.X, op=ALU.max, apply_absolute_value=True),
                        waits=[f"xn_{ch}_{s}"], fence=(s == 3))
                E.op("dve", lambda e: e.tensor_tensor(out=cw4(CAM), in0=cw4(CMX),
                     in1=cw4(CPR), op=ALU.mult), fence=True)
                E.op("dve", lambda e: e.tensor_scalar(out=cw4(CAMC), in0=cw4(CAM),
                     scalar1=1e-5, scalar2=None, op0=ALU.max), fence=True)
                E.op("dve", lambda e: e.reciprocal(out=cw4(CREC), in_=cw4(CAMC)), fence=True)
                E.op("dve", lambda e: e.tensor_scalar(out=cw4(CS127), in0=cw4(CREC),
                     scalar1=127.0, scalar2=None, op0=ALU.mult), fence=True)
                E.op("dve", lambda e: e.tensor_tensor(out=cw4(CT1), in0=cw4(CRSTD),
                     in1=cw4(CPR), op=ALU.mult), fence=True)
                E.op("dve", lambda e: e.tensor_tensor(out=cw4(CQQ), in0=cw4(CT1),
                     in1=cw4(CS127), op=ALU.mult), fence=True)
                E.op("dve", lambda e: e.tensor_tensor(out=cw4(CT2), in0=cw4(CM),
                     in1=cw4(CQQ), op=ALU.mult), fence=True)
                E.op("dve", lambda e: e.tensor_scalar(out=cw4(CBB2), in0=cw4(CT2),
                     scalar1=-1.0, scalar2=None, op0=ALU.mult), fence=True)
                E.op("dve", lambda e: e.tensor_tensor(
                    out=cw4(CKFN), in0=cw4(CAMC),
                    in1=prec[:, 17:18].broadcast_to([128, 4]), op=ALU.mult),
                    waits=["ws127"] + ([f"kcT_{ch - 1}"] if ch > 0 else []),
                    label=f"kfn_{ch}", fence=True)

            def u_q(E):
                for s in range(TSUB):
                    for hlf in range(2):
                        h8 = s * 2 + hlf
                        off = s * D + hlf * TC
                        qw = [f"kfn_{ch}"]
                        if h8 == 0 and ch > 0:
                            qw += [f"qT_{ch - 1}_{KF - 1}", f"abs_{ch - 1}_{KO - 1}",
                                   f"lnl_{ch - 1}_{KO - 1}"]
                        if h8 >= 2:
                            qw += [f"qm_{ch}_{h8 - 2}"]
                        E.op("act", lambda e, off=off, s=s, h8=h8: e.activation(
                            out=spa[:, 0:TC] if h8 % 2 == 0 else spb[:, 0:TC],
                            in_=xsc[:, off:off + TC], func=AF.Identity,
                            scale=scol4(CQQ, s), bias=scol4(CBB2, s)),
                            waits=qw, label=f"qs_{ch}_{h8}")
                        E.op("dve", lambda e, off=off, h8=h8: e.tensor_scalar(
                            out=qt[:, off:off + TC],
                            in0=spa[:, 0:TC] if h8 % 2 == 0 else spb[:, 0:TC],
                            scalar1=prec[:, 18:19], scalar2=prec[:, 19:20],
                            op0=ALU.add, op1=ALU.add),
                            waits=[f"qs_{ch}_{h8}"],
                            label=f"q_{ch}_{s}" if (s == 3 and hlf == 1) else f"qm_{ch}_{h8}")

            return ([(1 + s, lambda E, s=s: u_acc1(E, s)) for s in range(TSUB)] +
                    [(5 + s // 2, lambda E, s=s: u_acc2(E, s)) for s in range(TSUB)] +
                    [(7, u_cols), (8, u_xn), (9, u_q)])

        def td_post_units(ch):
            p = ch % 2

            def scol4(c, s2):
                return stats[:, c + s2:c + s2 + 1]

            units = []

            def u_xnt(E, kf):
                p2 = kf % 2
                tw = [f"xn_{ch}_3", "ld_fc"]
                if ch == 0 and kf == 0:
                    tw += ["wscp"]
                if kf >= 2:
                    tw += [f"xnbfev_{ch}_{kf - 2}"]
                    if kf - 2 < KO:
                        tw += [f"uev_{ch}_{kf - 2}"]
                elif ch >= 2:
                    tw += [f"outadd_{ch - 2}_3"]
                if kf < 2 and ch >= 1:
                    tw += [f"abs_{ch - 1}_{2 + p2}", f"relu_{ch - 1}_{2 + p2}"]
                for s in range(TSUB):
                    E.op("pe", lambda e, kf=kf, s=s, p2=p2: e.transpose(
                        out=ptr[p2][:, s * 128:(s + 1) * 128],
                        in_=xn[:, s * D + kf * 128: s * D + (kf + 1) * 128],
                        identity=eye[:]),
                        waits=tw if s == 0 else [],
                        label=f"xnT_{ch}_{kf}" if s == 3 else None)
                if kf < KO:
                    E.op("act", lambda e, kf=kf, p2=p2: e.mul(
                        out=uos(ch, kf), in_=ptr[p2][:], mul=sigtr[:, kf:kf + 1]),
                        waits=[f"xnT_{ch}_{kf}", "sigtr"] +
                              ([f"yinit_{ch - 2}_{kf}", f"du_{ch - 2}_{kf}"] if ch > 1 else []),
                        label=f"uev_{ch}_{kf}")
                E.op("act", lambda e, kf=kf, p2=p2: e.copy(
                    out=xnf[:, kf * TC:(kf + 1) * TC], in_=ptr[p2][:]),
                    waits=[f"xnT_{ch}_{kf}"] + ([f"xz_{ch - 1}"] if ch > 0 and kf < 2 else []),
                    label=f"xnbfev_{ch}_{kf}")

            for kf in range(KF):
                units.append(lambda E, kf=kf: u_xnt(E, kf))

            def u_qt(E, kf):
                qw = [f"q_{ch}_3", "ld_fc"]
                if kf >= 1:
                    qw += [f"qfev_{ch}_{kf - 1}"]
                elif ch > 0:
                    qw += [f"qfev_{ch - 1}_{KF - 1}"]
                for s in range(TSUB):
                    E.op("pe", lambda e, kf=kf, s=s: e.transpose(
                        out=pq[:, s * 128:(s + 1) * 128],
                        in_=qt[:, s * D + kf * 128: s * D + (kf + 1) * 128],
                        identity=eyeb[:]),
                        waits=qw if s == 0 else [],
                        label=f"qT_{ch}_{kf}" if s == 3 else None)
                E.op("act", lambda e, kf=kf: e.copy(
                    out=qf[:, kf * TC:(kf + 1) * TC], in_=pq[:, 0:TC]),
                    waits=[f"qT_{ch}_{kf}"] + ([f"ffn_{ch - 1}_3"] if ch > 0 and kf == 0 else []),
                    label=f"qfev_{ch}_{kf}")

            for kf in range(KF):
                units.append(lambda E, kf=kf: u_qt(E, kf))

            def u_xz(E, half):
                for kf in range(half * 4, half * 4 + 4):
                    zw = [f"xnbfev_{ch}_{kf}", f"xpwb_{kf}"]
                    if kf == 0:
                        zw += [f"xnbfev_{ch}_{KF - 2}"]
                    E.op("pe", lambda e, kf=kf: e.matmul(
                        out=ptr[0][0:96, :], lhsT=xpwb[:, kf * 96:(kf + 1) * 96],
                        rhs=xnf[:, kf * TC:(kf + 1) * TC],
                        start=(kf == 0), stop=(kf == KF - 1)),
                        waits=zw, label=f"xz_{ch}" if kf == KF - 1 else None)
                if half == 0:
                    return
                dpw = [f"xz_{ch}"]
                if ch > 0:
                    dpw += [f"dmm_{ch - 1}_3"]
                E.op("act", lambda e: e.copy(out=dpbf[:], in_=ptr[0][0:64, :]),
                     waits=dpw, label=f"dpbfev_{ch}")
                bw = [f"xz_{ch}"]
                if ch >= 2:
                    bw += [f"brep_{ch - 2}_1_15"]
                E.op("act", lambda e: e.copy(out=BCs[p][:], in_=ptr[0][64:96, :]),
                     waits=bw, label=f"bcev_{ch}")
                bw2 = [f"cstg_{ch - 2}"] if ch >= 2 else []
                E.op("act", lambda e: e.copy(out=BCsb[p][:], in_=ptr[0][64:96, :]),
                     waits=bw2, label=f"bcevb_{ch}")
                E.op("sync", lambda e: e.dma_start(
                    out=CSTGd[p], in_=BCsb[p][16:32, :]),
                    waits=[f"bcevb_{ch}"], label=f"cstg_{ch}", sem="cst", dinc=16)

            units.append(lambda E: u_xz(E, 0))
            units.append(lambda E: u_xz(E, 1))

            def u_dmm(E, ko):
                p2 = ko % 2
                dw = [f"dpbfev_{ch}", "dtwcast"]
                if ko < 2:
                    if p2 == 0:
                        dw += [f"bcevb_{ch}"]
                    else:
                        dw += [f"xnbfev_{ch}_{KF - 1}"]
                else:
                    dw += [f"abs_{ch}_{ko - 2}", f"relu_{ch}_{ko - 2}"]
                E.op("pe", lambda e, ko=ko, p2=p2: e.matmul(
                    out=ptr[p2][:], lhsT=dtwb[:, ko * 128:(ko + 1) * 128],
                    rhs=dpbf[:], start=True, stop=True),
                    waits=dw, label=f"dmm_{ch}_{ko}")
                dtb = colsb[:, ko:ko + 1]
                aw = [f"dmm_{ch}_{ko}", "ld_fc"]
                if ko == 0:
                    aw += [f"q_{ch}_3"]
                E.op("act", lambda e, p2=p2, dtb=dtb: e.activation(
                    out=spa[:], in_=ptr[p2][:], func=AF.Abs, bias=dtb),
                    waits=aw, label=f"abs_{ch}_{ko}")
                E.op("act", lambda e: e.activation(out=spb[:], in_=spa[:],
                     func=AF.Exp, scale=-1.0))

            def u_dmm2(E, ko):
                p2 = ko % 2
                dtb = colsb[:, ko:ko + 1]
                E.op("act", lambda e, ko=ko: e.activation(
                    out=xn[:, ko * TC:(ko + 1) * TC], in_=spb[:],
                    func=AF.Ln, bias=1.0), label=f"lnl_{ch}_{ko}")
                rw = [f"dmm_{ch}_{ko}", "ld_fc", f"abs_{ch}_{ko}"]
                if ch >= 2:
                    rw += [f"dA_{(ch - 2) * GPC + (ko // 2) * 32 + 30 + ko % 2}",
                           f"sgT_{ch - 2}_3"]
                E.op("act", lambda e, ch2=ch, ko=ko, p2=p2, dtb=dtb: e.activation(
                    out=dls(ch2, ko), in_=ptr[p2][:], func=AF.Relu, bias=dtb),
                    waits=rw, label=f"relu_{ch}_{ko}")

            for ko in range(KO):
                units.append(lambda E, ko=ko: u_dmm(E, ko))
                units.append(lambda E, ko=ko: u_dmm2(E, ko))

            return units

        def td_b(E, ch):
            def scol4(c, s2):
                return stats[:, c + s2:c + s2 + 1]

            for s2 in range(TSUB):
                kw = [f"kfn_{ch}", "ld_fc"]
                if s2 == 0:
                    if ch > 0:
                        kw += [f"krsbev_{ch - 1}", f"sg_{ch - 1}_3"]
                E.op("pe", lambda e, s2=s2: e.transpose(
                    out=pk[0:1, s2 * 128:(s2 + 1) * 128],
                    in_=scol4(CKFN, s2), identity=eye[:]),
                    waits=kw, label=f"kcT_{ch}" if s2 == 3 else None)
            E.op("act", lambda e: e.copy(out=spb[0:1, 0:TC], in_=pk[0:1, :]),
                 waits=[f"kcT_{ch}"], label=f"krowev_{ch}")
            E.op("pe", lambda e: e.matmul(out=pk[:], lhsT=ones1[:], rhs=spb[0:1, 0:TC],
                                          start=True, stop=True),
                 waits=[f"krowev_{ch}", "ld_fc"], label=f"k1_{ch}")
            E.op("act", lambda e: e.copy(out=krsb[:], in_=pk[:]),
                 waits=[f"k1_{ch}"], label=f"krsbev_{ch}")
            for ko in range(KO):
                E.op("dve", lambda e, ch2=ch, ko=ko: e.tensor_tensor(
                    out=dls(ch2, ko), in0=dls(ch2, ko),
                    in1=xn[:, ko * TC:(ko + 1) * TC], op=ALU.add),
                    waits=[f"lnl_{ch}_{ko}", f"relu_{ch}_{ko}"], label=f"spadd_{ch}_{ko}")
                duw = [f"spadd_{ch}_{ko}", f"uev_{ch}_{ko}"]
                if ch >= 2:
                    duw += [f"dbu_{(ch - 2) * GPC + (ko // 2) * 32 + 30 + ko % 2}",
                            f"odma_{ch - 2}"]
                E.op("dve", lambda e, ch2=ch, ko=ko: e.tensor_tensor(
                    out=dus(ch2, ko), in0=dls(ch2, ko), in1=uos(ch2, ko), op=ALU.mult),
                    waits=duw, label=f"du_{ch}_{ko}")
            for o in range(KO):
                fw = [f"qfev_{ch}_{KF - 1}", "ld_fc"]
                if o > 0:
                    fw += [f"ffnz_{ch}_{o - 1}"]
                elif ch > 0:
                    fw += [f"sg_{ch - 1}_2"]
                for kf in range(KF):
                    E.op("pe", lambda e, kf=kf, o=o: e.matmul(
                        out=pg[:], lhsT=wsb[:, (kf * KO + o) * 128:(kf * KO + o + 1) * 128],
                        rhs=qf[:, kf * TC:(kf + 1) * TC],
                        start=(kf == 0), stop=(kf == KF - 1)),
                        waits=fw if kf == 0 else [],
                        label=f"ffn_{ch}_{o}" if kf == KF - 1 else None)
                zw = [f"ffn_{ch}_{o}", f"krsbev_{ch}"]
                if ch > 0:
                    zw += [f"silu_{ch - 1}_{o}"]
                E.op("dve", lambda e, o=o: e.tensor_tensor(
                    out=sgbs(o), in0=pg[:], in1=krsb[:], op=ALU.mult),
                    waits=zw, label=f"ffnz_{ch}_{o}")
                sw = [f"ffnz_{ch}_{o}", "ld_fc"]
                if ch >= 2:
                    sw += [f"sg_{ch - 2}_{o}"]
                E.op("act", lambda e, ch2=ch, o=o: e.activation(
                    out=ggs(ch2, o), in_=sgbs(o), func=AF.Silu,
                    bias=colsb[:, 8 + o:9 + o]),
                    waits=sw, label=f"silu_{ch}_{o}")

        def emit_xin(E, ch):
            p = ch % 2
            if ch == 0:
                xw = []
            elif ch == 1:
                xw = [f"xpwb_{KF - 1}", "dtwcast"]
            else:
                xw = [f"outadd_{ch - 2}_3", f"sum_{ch - 2}"]
            E.op("sync", lambda e, p=p: e.dma_start(
                out=xs[p][:].rearrange("p (s d) -> p s d", d=D),
                in_=Xd[ch * TC:(ch + 1) * TC, :].rearrange("(s p) d -> p s d", p=128)),
                waits=xw, label=f"xin_{ch}", sem=f"xin{p}", dinc=16)

        def so_scan(E, ch, slotted):
            p = ch % 2
            G0 = ch * GPC
            for ko in range(2):
                yw = [f"uev_{ch}_{ko}", "ld_fc", "eyed"]
                yw += [f"ffnz_{ch}_{KO - 1}"] if ko == 0 else [f"krsbev_{ch}"]
                E.op("pe", lambda e, ch2=ch, ko=ko: e.matmul(
                    out=(pg if ko % 2 == 0 else pk)[:],
                    lhsT=eyed[:, ko * 128:(ko + 1) * 128], rhs=uos(ch2, ko),
                    start=True, stop=False),
                    waits=yw, label=f"yinit_{ch}_{ko}")
            for kop in range(2):
                for n in range(NST):
                    j = kop * NST + n
                    bw = [f"bcev_{ch}", "ld_fc"]
                    if j >= 3:
                        kp, np_ = (j - 3) // NST, (j - 3) % NST
                        bw += [f"dbu_{G0 + kp * 32 + np_ * 2 + 1}"]
                    elif ch > 0:
                        jj = 2 * NST - 3 + j
                        kp, np_ = jj // NST, jj % NST
                        bw += [f"dbu_{G0 - 64 + kp * 32 + np_ * 2 + 1}"]
                    E.op("pe", lambda e, n=n, j=j, p=p: e.matmul(
                        out=pb[j % 3][:], lhsT=id32[:, n:n + 1].broadcast_to([32, 128]),
                        rhs=BCs[p][:], start=True, stop=True),
                        waits=bw, label=f"brep_{ch}_{kop}_{n}")
                    cw2 = [f"cstg_{ch}"]
                    if n >= 4:
                        cw2 += [f"hc_{G0 + kop * 32 + (n - 4) * 2 + 1}"]
                    elif kop == 1:
                        cw2 += [f"hc_{G0 + (n + 12) * 2 + 1}"]
                    elif ch > 0:
                        cw2 += [f"hc_{G0 - 64 + 32 + (n + 12) * 2 + 1}"]
                    E.op("pool", lambda e, n=n, p=p: e.dma_start(
                        out=cring[:, (n % 4) * TC:(n % 4 + 1) * TC],
                        in_=CSTGd[p][n:n + 1, :].broadcast_to([128, TC])),
                        waits=cw2, label=f"cbc_{ch}_{kop}_{n}", sem="cq", dinc=16)
                    for koh in range(2):
                        ko = kop * 2 + koh
                        g = G0 + kop * 32 + n * 2 + koh
                        gp = g % 3
                        hho = hhs(g)
                        scol = state[:, ko * NST + n:ko * NST + n + 1]
                        aw = [f"spadd_{ch}_{ko}", "acoef"]
                        if g >= 3:
                            aw += [f"scan_{g - 3}"]
                        E.op("act", lambda e, ch2=ch, ko=ko, n=n, gp=gp: e.activation(
                            out=da[gp][:], in_=dls(ch2, ko), func=AF.Exp,
                            scale=acoef[:, ko * NST + n:ko * NST + n + 1]),
                            waits=aw, label=f"dA_{g}")
                        j2 = kop * NST + n
                        E.op("dve", lambda e, ch2=ch, ko=ko, gp=gp, j2=j2: e.tensor_tensor(
                            out=db[gp][:], in0=dus(ch2, ko), in1=pb[j2 % 3][:], op=ALU.mult),
                            waits=[f"brep_{ch}_{kop}_{n}", f"du_{ch}_{ko}"],
                            label=f"dbu_{g}")
                        if (g - G0) >= 1:
                            emit_hcmul(E, g - 1)
                        if kop == 1 and n == 1 and koh == 1:
                            for ko2 in range(2):
                                E.op("dve", lambda e, ch2=ch, ko2=ko2: e.tensor_tensor(
                                    out=dls(ch2, ko2), in0=(pg if ko2 == 0 else pk)[:],
                                    in1=ggs(ch2, ko2), op=ALU.mult),
                                    waits=[f"pacc_{G0 + 30 + ko2}", f"silu_{ch}_{ko2}"],
                                    label=f"sg_{ch}_{ko2}")
                            for ko2 in range(2, 4):
                                E.op("pe", lambda e, ch2=ch, ko2=ko2: e.matmul(
                                    out=(pg if ko2 % 2 == 0 else pk)[:],
                                    lhsT=eyed[:, ko2 * 128:(ko2 + 1) * 128],
                                    rhs=uos(ch2, ko2), start=True, stop=False),
                                    waits=[f"uev_{ch}_{ko2}", f"sg_{ch}_{ko2 - 2}"],
                                    label=f"yinit_{ch}_{ko2}")
                        if (g - G0) >= 3:
                            emit_pacc(E, g - 3)
                        sw = [f"dA_{g}"]
                        sw += [f"st_{g - NHH}"] if g >= NHH else ["st0"]
                        E.op("dve", lambda e, gp=gp, hho=hho, scol=scol: e.tensor_tensor_scan(
                            out=hho, data0=da[gp][:], data1=db[gp][:], initial=scol,
                            op0=ALU.mult, op1=ALU.add),
                            waits=sw, label=f"scan_{g}")
                        if (g - G0) >= 2:
                            emit_act_st(E, g - 2)
                    slot = kop * NST + n
                    for sl, fn in slotted:
                        if sl == slot:
                            fn(E)
            E.op("dve", lambda e: e.drain(), label=f"scanfin_{ch}")
            emit_hcmul(E, G0 + GPC - 1)
            for gt in range(G0 + GPC - 3, G0 + GPC):
                emit_pacc(E, gt)
            emit_act_st(E, G0 + GPC - 2)
            emit_act_st(E, G0 + GPC - 1)

        def so_out(E, ch):
            p = ch % 2
            G0 = ch * GPC
            for ko in range(2, KO):
                E.op("dve", lambda e, ch2=ch, ko=ko: e.tensor_tensor(
                    out=dls(ch2, ko), in0=(pg if ko % 2 == 0 else pk)[:],
                    in1=ggs(ch2, ko), op=ALU.mult),
                    waits=[f"pacc_{G0 + 62 + ko % 2}", f"silu_{ch}_{ko}"],
                    label=f"sg_{ch}_{ko}")
            for s in range(TSUB):
                p2 = s % 2
                tw = [f"sg_{ch}_3", "ld_fc"]
                if s >= 2:
                    tw += [f"outadd_{ch}_{s - 2}"]
                elif ch + 1 < nch:
                    tw += [f"abs_{ch + 1}_{2 + p2}", f"relu_{ch + 1}_{2 + p2}"]
                for ko in range(KO):
                    E.op("pe", lambda e, ch2=ch, s=s, ko=ko, p2=p2: e.transpose(
                        out=ptr[p2][:, ko * 128:(ko + 1) * 128],
                        in_=dl[ch2 % 2][:, ko * TC + s * 128: ko * TC + (s + 1) * 128],
                        identity=eye[:]),
                        waits=tw if ko == 0 else [],
                        label=f"sgT_{ch}_{s}" if ko == 3 else None)
                ow = [f"sgT_{ch}_{s}", f"sum_{ch}"]
                E.op("dve", lambda e, s=s, p2=p2, p=p: e.tensor_tensor(
                    out=du[p][:, s * DOWN:(s + 1) * DOWN], in0=ptr[p2][:],
                    in1=xs[p][:, s * D:s * D + DOWN], op=ALU.add),
                    waits=ow, label=f"outadd_{ch}_{s}")
            E.op("sync", lambda e, p=p: e.dma_start(
                out=Od[ch * TC:(ch + 1) * TC, :].rearrange("(s p) d -> p s d", p=128),
                in_=du[p][:].rearrange("p (s d) -> p s d", d=DOWN)),
                waits=[f"outadd_{ch}_3"], label=f"odma_{ch}", sem="oq", dinc=16)

        def prog(E):
            trcol = colsb[:, 12:20]
            E.op("sync", lambda e: e.dma_start(out=wsb[:], in_=Wd),
                 label="ld_w", sem="wq", dinc=16)
            E.op("sync", lambda e: e.dma_start(out=xs[1][:, 0:KF * 96], in_=XPWd),
                 label="ld_xpw", sem="wq", dinc=16)
            E.op("sync", lambda e: e.dma_start(
                out=xs[1][0:64, KF * 96:KF * 96 + DOWN], in_=DTWd),
                label="ld_dtw", sem="wq", dinc=16)
            for nm, dst, src in (("ld_alog", alog[:], ALOGd), ("ld_cols", colsb[:], COLSd),
                                 ("ld_ws", wssb[:], WSd), ("ld_sel", id32[:], ID32d),
                                 ("ld_eye", eye[:], EYEd), ("ld_eyeb", eyeb[:], EYEBd),
                                 ("ld_ones", ones1[:], ONESd),
                                 ("ld_fc", fc[:], FCd)):
                E.op("sync", lambda e, d=dst, s2=src: e.dma_start(out=d, in_=s2),
                     label=nm, sem="wq", dinc=16)

            E.op("dve", lambda e: e.tensor_copy(
                out=dtwb[:], in_=xs[1][0:64, KF * 96:KF * 96 + DOWN]),
                waits=["ld_fc"], label="dtwcast")
            E.op("act", lambda e: e.activation(out=prec[:, 0:8], in_=trcol,
                                               func=AF.Exp, scale=-1.0),
                 waits=["ld_fc"], label="etr", fence=True)
            E.op("dve", lambda e: e.tensor_scalar(out=prec[:, 8:16], in0=prec[:, 0:8],
                 scalar1=1.0, scalar2=None, op0=ALU.add), waits=["etr"], fence=True)
            E.op("dve", lambda e: e.reciprocal(out=sigtr[:], in_=prec[:, 8:16]),
                 label="sigtr", fence=True)
            for kf in range(KF):
                E.op("dve", lambda e, k=kf: e.tensor_scalar(
                    out=xpwb[:, k * 96:(k + 1) * 96], in0=xs[1][:, k * 96:(k + 1) * 96],
                    scalar1=sigtr[:, k:k + 1], scalar2=None, op0=ALU.mult),
                    waits=["ld_fc", "sigtr"], label=f"xpwb_{kf}",
                    fence=(kf == KF - 1))
            E.op("act", lambda e: e.activation(out=spa[:, 0:KO * NST], in_=alog[:], func=AF.Exp),
                 waits=["ld_fc"], label="aexpl", fence=True)
            E.op("dve", lambda e: e.tensor_scalar(out=acoef[:], in0=spa[:, 0:KO * NST],
                 scalar1=-1.0, scalar2=None, op0=ALU.mult),
                 waits=["aexpl"], label="acoef", fence=True)
            E.op("pe", lambda e: e.matmul(out=ptr[0][:, 0:1], lhsT=ones1[:], rhs=wssb[:],
                                          start=True, stop=True),
                 waits=["ld_fc"], label="wsbc")
            E.op("act", lambda e: e.copy(out=prec[:, 16:17], in_=ptr[0][:, 0:1]),
                 waits=["wsbc"], label="wscp", fence=True)
            E.op("dve", lambda e: e.tensor_scalar(out=prec[:, 17:18], in0=prec[:, 16:17],
                 scalar1=1.0 / 127.0, scalar2=None, op0=ALU.mult),
                 waits=["wscp"], label="ws127", fence=True)
            E.op("dve", lambda e: e.memset(state[:], 0.0), label="st0", fence=True)
            for ko in range(KO):
                E.op("dve", lambda e, ko=ko: e.tensor_tensor(
                    out=eyed[:, ko * 128:(ko + 1) * 128], in0=eye[:],
                    in1=colsb[:, 4 + ko:5 + ko].broadcast_to([128, 128]),
                    op=ALU.mult),
                    waits=["ld_fc"], label="eyed" if ko == KO - 1 else None,
                    fence=(ko == KO - 1))
            E.op("dve", lambda e: e.memset(prec[:, 19:20], 1.0), fence=True)
            E.op("dve", lambda e: e.tensor_scalar(
                out=prec[:, 18:19], in0=prec[:, 19:20],
                scalar1=MAGIC52, scalar2=-MAGIC52, op0=ALU.add, op1=ALU.add),
                fence=True)
            E.op("dve", lambda e: e.tensor_scalar(
                out=prec[:, 18:19], in0=prec[:, 18:19],
                scalar1=MAGDIFF, scalar2=MAGIC, op0=ALU.mult, op1=ALU.add),
                fence=True)
            E.op("dve", lambda e: e.tensor_scalar(
                out=prec[:, 19:20], in0=prec[:, 18:19],
                scalar1=-1.0, scalar2=None, op0=ALU.mult), label="mgc", fence=True)

            emit_xin(E, 0)
            emit_xin(E, 1)
            for _, u in td_pre_units(0):
                u(E)
            for u in td_post_units(0):
                u(E)
            td_b(E, 0)
            for ch in range(1, nch):
                slotted = td_pre_units(ch) + \
                    [(10 + (i * 19) // max(1, len(td_post_units(ch)) - 1), u)
                     for i, u in enumerate(td_post_units(ch))]
                so_scan(E, ch - 1, slotted)
                so_out(E, ch - 1)
                if ch + 1 < nch:
                    emit_xin(E, ch + 1)
                td_b(E, ch)
            so_scan(E, nch - 1, [])
            so_out(E, nch - 1)

            E.wait("sync", f"odma_{nch - 1}")

        ms = {}
        prog(Em(None, sems, ms))

        block = st.enter_context(nc.Block())

        def run_pass(name):
            def body(eng):
                prog(Em(name, sems, ms, eng=eng))
            return body

        block.sync(run_pass("sync"))
        block.tensor(run_pass("pe"))
        block.scalar(run_pass("act"))
        block.vector(run_pass("dve"))
        block.gpsimd(run_pass("pool"))

    return nc


_NC_CACHE = None


def _get_nc():
    global _NC_CACHE
    if _NC_CACHE is None:
        _NC_CACHE = build_nc()
    return _NC_CACHE


_SEL_CONST = None
_EYE_CONST = None
_EYEB_CONST = None
_ONES_CONST = None


def _consts():
    global _SEL_CONST, _EYE_CONST, _EYEB_CONST, _ONES_CONST
    if _SEL_CONST is None:
        _SEL_CONST = np.eye(32, dtype=np.float32)
        _EYE_CONST = np.eye(128, dtype=np.float32)
        _EYEB_CONST = np.eye(128, dtype=np.float32).astype(ml_dtypes.bfloat16)
        _ONES_CONST = np.ones((1, 128), dtype=np.float32)
    return _SEL_CONST, _EYE_CONST, _EYEB_CONST, _ONES_CONST


def _prep_core(x_b, j, A_log, x_proj_w, dt_proj_w, dt_proj_b, D_param, trend,
               w_int8, weight_scale, bias):
    own = np.arange(j * DOWN, (j + 1) * DOWN)
    oth = np.arange((1 - j) * DOWN, (1 - j) * DOWN + DOWN)
    idx = np.concatenate([own, oth])

    x_c = np.ascontiguousarray(x_b[:, idx], dtype=np.float32)
    xpwT = np.ascontiguousarray(
        x_proj_w[:, idx].T.reshape(KF, 128, 96).transpose(1, 0, 2)).reshape(128, KF * 96).astype(np.float32)
    dtwT = np.ascontiguousarray(dt_proj_w[own, :].T, dtype=np.float32)
    wT = w_int8[own][:, idx].T.astype(np.float32)
    wT = np.ascontiguousarray(
        wT.reshape(KF, 128, KO, 128).transpose(1, 0, 2, 3)).reshape(
        128, KF * KO * 128).astype(ml_dtypes.bfloat16)
    alog_c = np.ascontiguousarray(
        A_log[own].reshape(KO, 128, NST).transpose(1, 0, 2)).reshape(128, KO * NST).astype(np.float32)
    cols = np.zeros((128, 20), dtype=np.float32)
    cols[:, 0:4] = dt_proj_b[own].reshape(KO, 128).T
    cols[:, 4:8] = D_param[own].reshape(KO, 128).T
    cols[:, 8:12] = bias[own].reshape(KO, 128).T
    cols[:, 12:20] = trend[idx].reshape(KF, 128).T
    ws_c = np.asarray(weight_scale, dtype=np.float32).reshape(1, 1)
    sel, eye_f, eye_b, ones_r = _consts()
    fcv = np.tile(np.array([[1e-5, 1e-6, -0.5, 0.0]], dtype=np.float32), (128, 1))
    return dict(x=x_c, wt=wT, xpwt=xpwT, dtwt=dtwT, alog=alog_c, cols=cols,
                wsc=ws_c, id32=sel, eye=eye_f, eyeb=eye_b, ones1=ones_r,
                fconst=fcv)


def kernel(L0_attn_x, L0_attn_A_log, L0_attn_x_proj_w, L0_attn_dt_proj_w,
           L0_attn_dt_proj_b, L0_attn_D_param, L0_attn_trend_emphasis,
           L0_ffn_w_int8, L0_ffn_weight_scale, L0_ffn_bias):
    x = np.asarray(L0_attn_x, dtype=np.float32)
    A_log = np.asarray(L0_attn_A_log, dtype=np.float32)
    xpw = np.asarray(L0_attn_x_proj_w, dtype=np.float32)
    dtw = np.asarray(L0_attn_dt_proj_w, dtype=np.float32)
    dtb = np.asarray(L0_attn_dt_proj_b, dtype=np.float32)
    Dp = np.asarray(L0_attn_D_param, dtype=np.float32)
    tr = np.asarray(L0_attn_trend_emphasis, dtype=np.float32)
    w8 = np.asarray(L0_ffn_w_int8)
    ws = np.asarray(L0_ffn_weight_scale, dtype=np.float32)
    bi = np.asarray(L0_ffn_bias, dtype=np.float32)

    nc = _get_nc()
    in_maps = []
    for c in range(8):
        b, j = c // 2, c % 2
        in_maps.append(_prep_core(x[b], j, A_log, xpw, dtw, dtb, Dp, tr, w8, ws, bi))
    res = run_bass_kernel_spmd(nc, in_maps, core_ids=list(range(8)))
    out = np.empty((4, T, D), dtype=res.results[0]["o"].dtype)
    for c in range(8):
        b, j = c // 2, c % 2
        out[b][:, j * DOWN:(j + 1) * DOWN] = res.results[c]["o"]
    return out


# version 34
# speedup vs baseline: 1.0073x; 1.0073x over previous
import numpy as np
import ml_dtypes
import concourse.bass as bass
import concourse.mybir as mybir
from concourse.bass_utils import run_bass_kernel_spmd
from contextlib import ExitStack

F32 = mybir.dt.float32
BF16 = mybir.dt.bfloat16
AF = mybir.ActivationFunctionType
ALU = mybir.AluOpType
AX = mybir.AxisListType

T = 4096
D = 1024
DOWN = 512
NST = 16
TC = 512
TSUB = TC // 128
KF = D // 128
KO = DOWN // 128
NCH = T // TC
NHH = 5
GPC = KO * NST
MAGIC = float(1.5 * 2 ** 23)
MAGIC52 = float(1.5 * 2 ** 52)
MAGDIFF = float(1.5 * 2 ** 52 - 1.5 * 2 ** 23)

(CSUM, CSQ, CM, CMSQ, CMM, CVAR, CVEP, CRVP, CVR, CRSTD, CPR, CMX, CAM,
 CAMC, CREC, CS127, CT1, CQQ, CT2, CBB2, CKFN, CLNT, CLN2,
 CMR) = [4 * i for i in range(24)]


class Em:

    def __init__(self, name, sems, ms, eng=None):
        self.name = name
        self.sems = sems
        self.ms = ms
        self.eng = eng
        self.cnt = {k: 0 for k in sems}
        self.lastw = {}

    def op(self, queue, fn, waits=(), label=None, sem=None, dinc=1, fence=False):
        sem = sem or queue
        if self.name is None:
            for w in waits:
                if isinstance(w, tuple):
                    continue
                assert w.startswith("pcfree_") or w in self.ms, \
                    f"wait on undefined label {w}"
            if label is not None:
                self.cnt[sem] += dinc
                assert label not in self.ms, f"dup label {label}"
                self.ms[label] = (sem, self.cnt[sem])
            return
        if queue != self.name:
            return
        for w in waits:
            force = False
            if isinstance(w, tuple):
                w, m2 = w
                force = (m2 is True)
            sname, val = self.ms[w]
            if sname == self.name and not force:
                continue
            if self.lastw.get(sname, -1) >= val:
                continue
            self.eng.wait_ge(self.sems[sname], val)
            self.lastw[sname] = val
        inst = fn(self.eng)
        if fence:
            inst = self.eng.drain()
        if label is not None:
            sname, _ = self.ms[label]
            inst.then_inc(self.sems[sname], dinc)

    def wait(self, queue, label):
        if self.name is None:
            assert label in self.ms, label
            return
        if queue != self.name:
            return
        sname, val = self.ms[label]
        if self.lastw.get(sname, -1) >= val:
            return
        self.eng.wait_ge(self.sems[sname], val)
        self.lastw[sname] = val


def build_nc(nch=NCH):
    nc = bass.Bass("TRN2", target_bir_lowering=False, debug=False,
                   detect_race_conditions=False)

    Xd = nc.dram_tensor("x", [T, D], F32, kind="ExternalInput").ap()
    Wd = nc.dram_tensor("wt", [128, KF * KO * 128], BF16, kind="ExternalInput").ap()
    XPWd = nc.dram_tensor("xpwt", [128, KF * 96], F32, kind="ExternalInput").ap()
    DTWd = nc.dram_tensor("dtwt", [64, DOWN], F32, kind="ExternalInput").ap()
    ALOGd = nc.dram_tensor("alog", [128, KO * NST], F32, kind="ExternalInput").ap()
    COLSd = nc.dram_tensor("cols", [128, 20], F32, kind="ExternalInput").ap()
    WSd = nc.dram_tensor("wsc", [1, 1], F32, kind="ExternalInput").ap()
    ID32d = nc.dram_tensor("id32", [32, 32], F32, kind="ExternalInput").ap()
    EYEd = nc.dram_tensor("eye", [128, 128], F32, kind="ExternalInput").ap()
    EYEBd = nc.dram_tensor("eyeb", [128, 128], BF16, kind="ExternalInput").ap()
    ONESd = nc.dram_tensor("ones1", [1, 128], F32, kind="ExternalInput").ap()
    FCd = nc.dram_tensor("fconst", [128, 4], F32, kind="ExternalInput").ap()
    CSTGd = nc.dram_tensor("cstg", [2, NST, TC], BF16, kind="Internal").ap()
    Od = nc.dram_tensor("o", [T, DOWN], F32, kind="ExternalOutput").ap()

    with ExitStack() as st:
        def sb(name, shape, dt=F32):
            return st.enter_context(nc.sbuf_tensor(name, shape, dt))

        def ps(name, shape, dt=F32):
            return st.enter_context(nc.psum_tensor(name, shape, dt))

        xs = [sb("xs0", [128, TSUB * D]), sb("xs1", [128, TSUB * D])]
        xn = sb("xn", [128, TSUB * D])
        qt = sb("qt", [128, TSUB * D], BF16)
        xnf = sb("xnf", [128, KF * TC], BF16)
        qf = sb("qf", [128, KF * TC], BF16)
        uo = [sb("uo0", [128, KO * TC]), sb("uo1", [128, KO * TC])]
        dl = [sb("dl0", [128, KO * TC]), sb("dl1", [128, KO * TC])]
        du = [sb("du0", [128, KO * TC]), sb("du1", [128, KO * TC])]
        gg = [sb("gg0", [128, KO * TC]), sb("gg1", [128, KO * TC])]
        BCs = [sb("BCs0", [32, TC]), sb("BCs1", [32, TC])]
        sgb = sb("sgb", [128, KO * TC])
        dpbf = sb("dpbf", [64, TC], BF16)
        da = [sb("da0", [128, TC]), sb("da1", [128, TC]), sb("da2", [128, TC])]
        db = [sb("db0", [128, TC]), sb("db1", [128, TC]), sb("db2", [128, TC])]
        hh = sb("hh", [128, NHH * TC])
        hcD = sb("hcD", [128, 6 * TC], BF16)
        eyed = sb("eyed", [128, KO * 128])
        BCsb = [sb("bcsb0", [32, TC], BF16), sb("bcsb1", [32, TC], BF16)]
        cring = sb("cring", [128, 4 * TC], BF16)
        spa = sb("spa", [128, TC])
        spb = sb("spb", [128, TC])
        wsb = sb("wsb", [128, KF * KO * 128], BF16)
        xpwb = sb("xpwb", [128, KF * 96], BF16)
        dtwb = sb("dtwb", [64, DOWN], BF16)
        alog = sb("alogs", [128, KO * NST])
        acoef = sb("acoefs", [128, KO * NST])
        colsb = sb("colsb", [128, 20])
        stats = sb("stats", [128, 96])
        prec = sb("prec", [128, 24])
        sigtr = sb("sigtrs", [128, KF])
        state = sb("state", [128, KO * NST])
        id32 = sb("id32s", [32, 32])
        eye = sb("eyes", [128, 128])
        eyeb = sb("eyebs", [128, 128], BF16)
        ones1 = sb("ones1s", [1, 128])
        wssb = sb("wssb", [1, 1])
        krsb = sb("krsb", [128, TC])
        fc = sb("fcs", [128, 4])
        nc.const_aps.register(F32, 1e-5, fc[:, 0:1])
        nc.const_aps.register(F32, 1e-6, fc[:, 1:2])
        nc.const_aps.register(F32, -0.5, fc[:, 2:3])

        ptr = [ps("ptr0", [128, TC]), ps("ptr1", [128, TC])]
        pq = ps("pq", [128, 2 * TC], BF16)
        pg = ps("pg", [128, TC])
        pb = [ps("pb0", [128, TC]), ps("pb1", [128, TC]), ps("pb2", [128, TC])]
        pk = ps("pk", [128, TC])

        sems = {}
        for s_ in ("pe", "act", "dve", "pool", "xin0", "xin1", "oq", "wq",
                   "cst", "cq"):
            sems[s_] = st.enter_context(nc.semaphore("sem_" + s_))

        def dls(ch, ko):
            return dl[ch % 2][:, ko * TC:(ko + 1) * TC]

        def dus(ch, ko):
            return du[ch % 2][:, ko * TC:(ko + 1) * TC]

        def uos(ch, ko):
            return uo[ch % 2][:, ko * TC:(ko + 1) * TC]

        def ggs(ch, ko):
            return gg[ch % 2][:, ko * TC:(ko + 1) * TC]

        def sgbs(ko):
            return sgb[:, ko * TC:(ko + 1) * TC]

        def hhs(g):
            ghp = g % NHH
            return hh[:, ghp * TC:(ghp + 1) * TC]

        def emit_hcmul(E, g):
            ch = g // GPC
            r = g - ch * GPC
            kop, n, koh = r // 32, (r % 32) // 2, r % 2
            hho = hhs(g)
            hcd = hcD[:, (g % 6) * TC:(g % 6 + 1) * TC]
            hw = [f"cbc_{ch}_{kop}_{n}"]
            if g >= 6:
                hw += [f"pacc_{g - 6}"]
            E.op("dve", lambda e, hho=hho, hcd=hcd, n=n: e.tensor_tensor(
                out=hcd, in0=hho, in1=cring[:, (n % 4) * TC:(n % 4 + 1) * TC],
                op=ALU.mult),
                waits=hw, label=f"hc_{g}")

        def emit_pacc(E, g):
            r = g - (g // GPC) * GPC
            kop, n, koh = r // 32, (r % 32) // 2, r % 2
            hcd = hcD[:, (g % 6) * TC:(g % 6 + 1) * TC]
            bank = pg if koh == 0 else pk
            E.op("pe", lambda e, hcd=hcd, bank=bank: e.matmul(
                out=bank[:], lhsT=eyeb[:], rhs=hcd,
                start=False, stop=(n == NST - 1)),
                waits=[f"hc_{g}"], label=f"pacc_{g}")

        def emit_act_st(E, g):
            ch = g // GPC
            r = g - ch * GPC
            kop, n, koh = r // 32, (r % 32) // 2, r % 2
            ko = kop * 2 + koh
            hho = hhs(g)
            scol = state[:, ko * NST + n:ko * NST + n + 1]
            stw = [f"scan_{g}", "st0"]
            if (g - ch * GPC) < GPC - 1:
                stw += [f"dbu_{g + 1}"]
            else:
                stw += [f"scanfin_{ch}"]
            E.op("act", lambda e, hho=hho, scol=scol: e.copy(
                out=scol, in_=hho[:, TC - 1:TC]),
                waits=stw, label=f"st_{g}",
                fence=((g - ch * GPC) == GPC - 1))

        def td_pre_units(ch):
            p = ch % 2
            xsc = xs[p]

            def xss(s):
                return xsc[:, s * D:(s + 1) * D]

            def cw4(c):
                return stats[:, c:c + 4]

            def scol4(c, s):
                return stats[:, c + s:c + s + 1]

            def u_acc1(E, s):
                sw_ = [f"xin_{ch}"]
                if s == 0:
                    sw_ += [f"var_{ch - 1}"] if ch > 0 else []
                    sw_ += [f"odma_{ch - 2}"] if ch > 1 else []
                E.op("act", lambda e, s=s: e.activation(
                    out=du[p][:, 0:D], in_=xss(s), func=AF.Copy,
                    accum_out=scol4(CSUM, s)),
                    waits=sw_, label=f"sum_{ch}" if s == 3 else None, fence=(s == 3))

            def u_acc2(E, s):
                E.op("act", lambda e, s=s: e.activation(
                    out=du[p][:, 0:D], in_=xss(s), func=AF.Square,
                    accum_out=scol4(CSQ, s)),
                    label=f"sq_{ch}_{s}", fence=(s == 3))

            def u_cols(E):
                mw = [f"sum_{ch}"] + ([f"prs_{ch - 1}"] if ch > 0 else [])
                E.op("dve", lambda e: e.tensor_scalar(out=cw4(CM), in0=cw4(CSUM),
                     scalar1=1.0 / D, scalar2=None, op0=ALU.mult), waits=mw, fence=True)
                E.op("dve", lambda e: e.tensor_scalar(out=cw4(CMSQ), in0=cw4(CSQ),
                     scalar1=1.0 / D, scalar2=None, op0=ALU.mult),
                     waits=[f"sq_{ch}_3"], fence=True)
                E.op("dve", lambda e: e.tensor_tensor(out=cw4(CMM), in0=cw4(CM),
                     in1=cw4(CM), op=ALU.mult), fence=True)
                E.op("dve", lambda e: e.tensor_tensor(out=cw4(CVAR), in0=cw4(CMSQ),
                     in1=cw4(CMM), op=ALU.subtract), fence=True)
                E.op("dve", lambda e: e.tensor_scalar(out=cw4(CVEP), in0=cw4(CVAR),
                     scalar1=1e-5, scalar2=None, op0=ALU.add), fence=True)
                E.op("dve", lambda e: e.reciprocal(out=cw4(CRVP), in_=cw4(CVEP)), fence=True)
                E.op("dve", lambda e: e.tensor_tensor(out=cw4(CVR), in0=cw4(CVAR),
                     in1=cw4(CRVP), op=ALU.mult), label=f"var_{ch}", fence=True)
                E.op("act", lambda e: e.activation(out=cw4(CLNT), in_=cw4(CVAR),
                     func=AF.Ln, bias=1e-5),
                     waits=[f"var_{ch}", "ld_fc"] + ([f"q_{ch - 1}_3"] if ch > 0 else []),
                     fence=True)
                E.op("act", lambda e: e.activation(out=cw4(CRSTD), in_=cw4(CLNT),
                     func=AF.Exp, scale=-0.5), fence=True)
                E.op("act", lambda e: e.activation(out=cw4(CLN2), in_=cw4(CVR),
                     func=AF.Ln, bias=1e-6), fence=True)
                E.op("act", lambda e: e.activation(out=cw4(CPR), in_=cw4(CLN2),
                     func=AF.Exp, scale=-0.5), label=f"prs_{ch}", fence=True)

            def u_xn(E):
                E.op("dve", lambda e: e.scalar_tensor_tensor(
                    out=cw4(CMR), in0=cw4(CM), scalar=-1.0, in1=cw4(CRSTD),
                    op0=ALU.mult, op1=ALU.mult),
                    waits=[f"prs_{ch}"], label=f"cmr_{ch}", fence=True)
                for s in range(TSUB):
                    xw = [f"cmr_{ch}"]
                    if ch > 0 and s == 0:
                        xw += [f"xnT_{ch - 1}_{KF - 1}", f"spadd_{ch - 1}_{KO - 1}"]
                    E.op("act", lambda e, s=s: e.activation(
                        out=xn[:, s * D:(s + 1) * D], in_=xss(s), func=AF.Identity,
                        scale=scol4(CRSTD, s), bias=scol4(CMR, s)),
                        waits=xw, label=f"xn_{ch}_{s}")
                    E.op("dve", lambda e, s=s: e.tensor_reduce(
                        out=scol4(CMX, s), in_=xn[:, s * D:(s + 1) * D],
                        axis=AX.X, op=ALU.max, apply_absolute_value=True),
                        waits=[f"xn_{ch}_{s}"], fence=(s == 3))
                E.op("dve", lambda e: e.tensor_tensor(out=cw4(CAM), in0=cw4(CMX),
                     in1=cw4(CPR), op=ALU.mult), fence=True)
                E.op("dve", lambda e: e.tensor_scalar(out=cw4(CAMC), in0=cw4(CAM),
                     scalar1=1e-5, scalar2=None, op0=ALU.max), fence=True)
                E.op("dve", lambda e: e.reciprocal(out=cw4(CREC), in_=cw4(CAMC)), fence=True)
                E.op("dve", lambda e: e.tensor_scalar(out=cw4(CS127), in0=cw4(CREC),
                     scalar1=127.0, scalar2=None, op0=ALU.mult), fence=True)
                E.op("dve", lambda e: e.tensor_tensor(out=cw4(CT1), in0=cw4(CRSTD),
                     in1=cw4(CPR), op=ALU.mult), fence=True)
                E.op("dve", lambda e: e.tensor_tensor(out=cw4(CQQ), in0=cw4(CT1),
                     in1=cw4(CS127), op=ALU.mult), fence=True)
                E.op("dve", lambda e: e.tensor_tensor(out=cw4(CT2), in0=cw4(CM),
                     in1=cw4(CQQ), op=ALU.mult), fence=True)
                E.op("dve", lambda e: e.tensor_scalar(out=cw4(CBB2), in0=cw4(CT2),
                     scalar1=-1.0, scalar2=None, op0=ALU.mult), fence=True)
                E.op("dve", lambda e: e.tensor_tensor(
                    out=cw4(CKFN), in0=cw4(CAMC),
                    in1=prec[:, 17:18].broadcast_to([128, 4]), op=ALU.mult),
                    waits=["ws127"] + ([f"kcT_{ch - 1}"] if ch > 0 else []),
                    label=f"kfn_{ch}", fence=True)

            def u_q(E):
                for s in range(TSUB):
                    for hlf in range(2):
                        h8 = s * 2 + hlf
                        off = s * D + hlf * TC
                        qw = [f"kfn_{ch}"]
                        if h8 == 0 and ch > 0:
                            qw += [f"qT_{ch - 1}_{KF - 1}", f"abs_{ch - 1}_{KO - 1}",
                                   f"lnl_{ch - 1}_{KO - 1}"]
                        if h8 >= 2:
                            qw += [f"qm_{ch}_{h8 - 2}"]
                        E.op("act", lambda e, off=off, s=s, h8=h8: e.activation(
                            out=spa[:, 0:TC] if h8 % 2 == 0 else spb[:, 0:TC],
                            in_=xsc[:, off:off + TC], func=AF.Identity,
                            scale=scol4(CQQ, s), bias=scol4(CBB2, s)),
                            waits=qw, label=f"qs_{ch}_{h8}")
                        E.op("dve", lambda e, off=off, h8=h8: e.tensor_scalar(
                            out=qt[:, off:off + TC],
                            in0=spa[:, 0:TC] if h8 % 2 == 0 else spb[:, 0:TC],
                            scalar1=prec[:, 18:19], scalar2=prec[:, 19:20],
                            op0=ALU.add, op1=ALU.add),
                            waits=[f"qs_{ch}_{h8}"],
                            label=f"q_{ch}_{s}" if (s == 3 and hlf == 1) else f"qm_{ch}_{h8}")

            return ([(1 + s, lambda E, s=s: u_acc1(E, s)) for s in range(TSUB)] +
                    [(4 + s // 2, lambda E, s=s: u_acc2(E, s)) for s in range(TSUB)] +
                    [(6, u_cols), (7, u_xn), (8, u_q)])

        def td_post_units(ch):
            p = ch % 2

            def scol4(c, s2):
                return stats[:, c + s2:c + s2 + 1]

            units = []

            def u_xnt(E, kf):
                p2 = kf % 2
                tw = [f"xn_{ch}_3", "ld_fc"]
                if ch == 0 and kf == 0:
                    tw += ["wscp"]
                if kf >= 2:
                    tw += [f"xnbfev_{ch}_{kf - 2}"]
                    if kf - 2 < KO:
                        tw += [f"uev_{ch}_{kf - 2}"]
                elif ch >= 2:
                    tw += [f"outadd_{ch - 2}_3"]
                if kf < 2 and ch >= 1:
                    tw += [f"abs_{ch - 1}_{2 + p2}", f"relu_{ch - 1}_{2 + p2}"]
                for s in range(TSUB):
                    E.op("pe", lambda e, kf=kf, s=s, p2=p2: e.transpose(
                        out=ptr[p2][:, s * 128:(s + 1) * 128],
                        in_=xn[:, s * D + kf * 128: s * D + (kf + 1) * 128],
                        identity=eye[:]),
                        waits=tw if s == 0 else [],
                        label=f"xnT_{ch}_{kf}" if s == 3 else None)
                if kf < KO:
                    E.op("act", lambda e, kf=kf, p2=p2: e.mul(
                        out=uos(ch, kf), in_=ptr[p2][:], mul=sigtr[:, kf:kf + 1]),
                        waits=[f"xnT_{ch}_{kf}", "sigtr"] +
                              ([f"yinit_{ch - 2}_{kf}", f"du_{ch - 2}_{kf}"] if ch > 1 else []),
                        label=f"uev_{ch}_{kf}")
                E.op("act", lambda e, kf=kf, p2=p2: e.copy(
                    out=xnf[:, kf * TC:(kf + 1) * TC], in_=ptr[p2][:]),
                    waits=[f"xnT_{ch}_{kf}"] + ([f"xz_{ch - 1}"] if ch > 0 and kf < 2 else []),
                    label=f"xnbfev_{ch}_{kf}")

            for kf in range(KF):
                units.append(lambda E, kf=kf: u_xnt(E, kf))

            def u_qt(E, kf):
                qw = [f"q_{ch}_3", "ld_fc"]
                if kf >= 1:
                    qw += [f"qfev_{ch}_{kf - 1}"]
                elif ch > 0:
                    qw += [f"qfev_{ch - 1}_{KF - 1}"]
                for s in range(TSUB):
                    E.op("pe", lambda e, kf=kf, s=s: e.transpose(
                        out=pq[:, s * 128:(s + 1) * 128],
                        in_=qt[:, s * D + kf * 128: s * D + (kf + 1) * 128],
                        identity=eyeb[:]),
                        waits=qw if s == 0 else [],
                        label=f"qT_{ch}_{kf}" if s == 3 else None)
                E.op("act", lambda e, kf=kf: e.copy(
                    out=qf[:, kf * TC:(kf + 1) * TC], in_=pq[:, 0:TC]),
                    waits=[f"qT_{ch}_{kf}"] + ([f"ffn_{ch - 1}_3"] if ch > 0 and kf == 0 else []),
                    label=f"qfev_{ch}_{kf}")

            for kf in range(KF):
                units.append(lambda E, kf=kf: u_qt(E, kf))

            def u_xz(E, half):
                for kf in range(half * 4, half * 4 + 4):
                    zw = [f"xnbfev_{ch}_{kf}", f"xpwb_{kf}"]
                    if kf == 0:
                        zw += [f"xnbfev_{ch}_{KF - 2}"]
                    E.op("pe", lambda e, kf=kf: e.matmul(
                        out=ptr[0][0:96, :], lhsT=xpwb[:, kf * 96:(kf + 1) * 96],
                        rhs=xnf[:, kf * TC:(kf + 1) * TC],
                        start=(kf == 0), stop=(kf == KF - 1)),
                        waits=zw, label=f"xz_{ch}" if kf == KF - 1 else None)
                if half == 0:
                    return
                dpw = [f"xz_{ch}"]
                if ch > 0:
                    dpw += [f"dmm_{ch - 1}_3"]
                E.op("act", lambda e: e.copy(out=dpbf[:], in_=ptr[0][0:64, :]),
                     waits=dpw, label=f"dpbfev_{ch}")
                bw = [f"xz_{ch}"]
                if ch >= 2:
                    bw += [f"brep_{ch - 2}_1_15"]
                E.op("act", lambda e: e.copy(out=BCs[p][:], in_=ptr[0][64:96, :]),
                     waits=bw, label=f"bcev_{ch}")
                bw2 = [f"cstg_{ch - 2}"] if ch >= 2 else []
                E.op("act", lambda e: e.copy(out=BCsb[p][:], in_=ptr[0][64:96, :]),
                     waits=bw2, label=f"bcevb_{ch}")
                E.op("sync", lambda e: e.dma_start(
                    out=CSTGd[p], in_=BCsb[p][16:32, :]),
                    waits=[f"bcevb_{ch}"], label=f"cstg_{ch}", sem="cst", dinc=16)

            units.append(lambda E: u_xz(E, 0))
            units.append(lambda E: u_xz(E, 1))

            def u_dmm(E, ko):
                p2 = ko % 2
                dw = [f"dpbfev_{ch}", "dtwcast"]
                if ko < 2:
                    if p2 == 0:
                        dw += [f"bcevb_{ch}"]
                    else:
                        dw += [f"xnbfev_{ch}_{KF - 1}"]
                else:
                    dw += [f"abs_{ch}_{ko - 2}", f"relu_{ch}_{ko - 2}"]
                E.op("pe", lambda e, ko=ko, p2=p2: e.matmul(
                    out=ptr[p2][:], lhsT=dtwb[:, ko * 128:(ko + 1) * 128],
                    rhs=dpbf[:], start=True, stop=True),
                    waits=dw, label=f"dmm_{ch}_{ko}")
                dtb = colsb[:, ko:ko + 1]
                aw = [f"dmm_{ch}_{ko}", "ld_fc"]
                if ko == 0:
                    aw += [f"q_{ch}_3"]
                E.op("act", lambda e, p2=p2, dtb=dtb: e.activation(
                    out=spa[:], in_=ptr[p2][:], func=AF.Abs, bias=dtb),
                    waits=aw, label=f"abs_{ch}_{ko}")
                E.op("act", lambda e: e.activation(out=spb[:], in_=spa[:],
                     func=AF.Exp, scale=-1.0))

            def u_dmm2(E, ko):
                p2 = ko % 2
                dtb = colsb[:, ko:ko + 1]
                E.op("act", lambda e, ko=ko: e.activation(
                    out=xn[:, ko * TC:(ko + 1) * TC], in_=spb[:],
                    func=AF.Ln, bias=1.0), label=f"lnl_{ch}_{ko}")
                rw = [f"dmm_{ch}_{ko}", "ld_fc", f"abs_{ch}_{ko}"]
                if ch >= 2:
                    rw += [f"dA_{(ch - 2) * GPC + (ko // 2) * 32 + 30 + ko % 2}",
                           f"sgT_{ch - 2}_3"]
                E.op("act", lambda e, ch2=ch, ko=ko, p2=p2, dtb=dtb: e.activation(
                    out=dls(ch2, ko), in_=ptr[p2][:], func=AF.Relu, bias=dtb),
                    waits=rw, label=f"relu_{ch}_{ko}")

            for ko in range(KO):
                units.append(lambda E, ko=ko: u_dmm(E, ko))
                units.append(lambda E, ko=ko: u_dmm2(E, ko))

            return units

        def td_b(E, ch):
            def scol4(c, s2):
                return stats[:, c + s2:c + s2 + 1]

            for s2 in range(TSUB):
                kw = [f"kfn_{ch}", "ld_fc"]
                if s2 == 0:
                    if ch > 0:
                        kw += [f"krsbev_{ch - 1}", f"sg_{ch - 1}_3"]
                E.op("pe", lambda e, s2=s2: e.transpose(
                    out=pk[0:1, s2 * 128:(s2 + 1) * 128],
                    in_=scol4(CKFN, s2), identity=eye[:]),
                    waits=kw, label=f"kcT_{ch}" if s2 == 3 else None)
            E.op("act", lambda e: e.copy(out=spb[0:1, 0:TC], in_=pk[0:1, :]),
                 waits=[f"kcT_{ch}"], label=f"krowev_{ch}")
            E.op("pe", lambda e: e.matmul(out=pk[:], lhsT=ones1[:], rhs=spb[0:1, 0:TC],
                                          start=True, stop=True),
                 waits=[f"krowev_{ch}", "ld_fc"], label=f"k1_{ch}")
            E.op("act", lambda e: e.copy(out=krsb[:], in_=pk[:]),
                 waits=[f"k1_{ch}"], label=f"krsbev_{ch}")
            for ko in range(KO):
                E.op("dve", lambda e, ch2=ch, ko=ko: e.tensor_tensor(
                    out=dls(ch2, ko), in0=dls(ch2, ko),
                    in1=xn[:, ko * TC:(ko + 1) * TC], op=ALU.add),
                    waits=[f"lnl_{ch}_{ko}", f"relu_{ch}_{ko}"], label=f"spadd_{ch}_{ko}")
                duw = [f"spadd_{ch}_{ko}", f"uev_{ch}_{ko}"]
                if ch >= 2:
                    duw += [f"dbu_{(ch - 2) * GPC + (ko // 2) * 32 + 30 + ko % 2}",
                            f"odma_{ch - 2}"]
                E.op("dve", lambda e, ch2=ch, ko=ko: e.tensor_tensor(
                    out=dus(ch2, ko), in0=dls(ch2, ko), in1=uos(ch2, ko), op=ALU.mult),
                    waits=duw, label=f"du_{ch}_{ko}")
            for o in range(KO):
                fw = [f"qfev_{ch}_{KF - 1}", "ld_fc"]
                if o > 0:
                    fw += [f"ffnz_{ch}_{o - 1}"]
                elif ch > 0:
                    fw += [f"sg_{ch - 1}_2"]
                for kf in range(KF):
                    E.op("pe", lambda e, kf=kf, o=o: e.matmul(
                        out=pg[:], lhsT=wsb[:, (kf * KO + o) * 128:(kf * KO + o + 1) * 128],
                        rhs=qf[:, kf * TC:(kf + 1) * TC],
                        start=(kf == 0), stop=(kf == KF - 1)),
                        waits=fw if kf == 0 else [],
                        label=f"ffn_{ch}_{o}" if kf == KF - 1 else None)
                zw = [f"ffn_{ch}_{o}", f"krsbev_{ch}"]
                if ch > 0:
                    zw += [f"silu_{ch - 1}_{o}"]
                E.op("dve", lambda e, o=o: e.tensor_tensor(
                    out=sgbs(o), in0=pg[:], in1=krsb[:], op=ALU.mult),
                    waits=zw, label=f"ffnz_{ch}_{o}")
                sw = [f"ffnz_{ch}_{o}", "ld_fc"]
                if ch >= 2:
                    sw += [f"sg_{ch - 2}_{o}"]
                E.op("act", lambda e, ch2=ch, o=o: e.activation(
                    out=ggs(ch2, o), in_=sgbs(o), func=AF.Silu,
                    bias=colsb[:, 8 + o:9 + o]),
                    waits=sw, label=f"silu_{ch}_{o}")

        def emit_xin(E, ch):
            p = ch % 2
            if ch == 0:
                xw = []
            elif ch == 1:
                xw = [f"xpwb_{KF - 1}", "dtwcast"]
            else:
                xw = [f"outadd_{ch - 2}_3", f"sum_{ch - 2}"]
            E.op("sync", lambda e, p=p: e.dma_start(
                out=xs[p][:].rearrange("p (s d) -> p s d", d=D),
                in_=Xd[ch * TC:(ch + 1) * TC, :].rearrange("(s p) d -> p s d", p=128)),
                waits=xw, label=f"xin_{ch}", sem=f"xin{p}", dinc=16)

        def so_scan(E, ch, slotted):
            p = ch % 2
            G0 = ch * GPC
            for ko in range(2):
                yw = [f"uev_{ch}_{ko}", "ld_fc", "eyed"]
                yw += [f"ffnz_{ch}_{KO - 1}"] if ko == 0 else [f"krsbev_{ch}"]
                E.op("pe", lambda e, ch2=ch, ko=ko: e.matmul(
                    out=(pg if ko % 2 == 0 else pk)[:],
                    lhsT=eyed[:, ko * 128:(ko + 1) * 128], rhs=uos(ch2, ko),
                    start=True, stop=False),
                    waits=yw, label=f"yinit_{ch}_{ko}")
            for kop in range(2):
                for n in range(NST):
                    j = kop * NST + n
                    bw = [f"bcev_{ch}", "ld_fc"]
                    if j >= 3:
                        kp, np_ = (j - 3) // NST, (j - 3) % NST
                        bw += [f"dbu_{G0 + kp * 32 + np_ * 2 + 1}"]
                    elif ch > 0:
                        jj = 2 * NST - 3 + j
                        kp, np_ = jj // NST, jj % NST
                        bw += [f"dbu_{G0 - 64 + kp * 32 + np_ * 2 + 1}"]
                    E.op("pe", lambda e, n=n, j=j, p=p: e.matmul(
                        out=pb[j % 3][:], lhsT=id32[:, n:n + 1].broadcast_to([32, 128]),
                        rhs=BCs[p][:], start=True, stop=True),
                        waits=bw, label=f"brep_{ch}_{kop}_{n}")
                    cw2 = [f"cstg_{ch}"]
                    if n >= 4:
                        cw2 += [f"hc_{G0 + kop * 32 + (n - 4) * 2 + 1}"]
                    elif kop == 1:
                        cw2 += [f"hc_{G0 + (n + 12) * 2 + 1}"]
                    elif ch > 0:
                        cw2 += [f"hc_{G0 - 64 + 32 + (n + 12) * 2 + 1}"]
                    E.op("pool", lambda e, n=n, p=p: e.dma_start(
                        out=cring[:, (n % 4) * TC:(n % 4 + 1) * TC],
                        in_=CSTGd[p][n:n + 1, :].broadcast_to([128, TC])),
                        waits=cw2, label=f"cbc_{ch}_{kop}_{n}", sem="cq", dinc=16)
                    for koh in range(2):
                        ko = kop * 2 + koh
                        g = G0 + kop * 32 + n * 2 + koh
                        gp = g % 3
                        hho = hhs(g)
                        scol = state[:, ko * NST + n:ko * NST + n + 1]
                        aw = [f"spadd_{ch}_{ko}", "acoef"]
                        if g >= 3:
                            aw += [f"scan_{g - 3}"]
                        E.op("act", lambda e, ch2=ch, ko=ko, n=n, gp=gp: e.activation(
                            out=da[gp][:], in_=dls(ch2, ko), func=AF.Exp,
                            scale=acoef[:, ko * NST + n:ko * NST + n + 1]),
                            waits=aw, label=f"dA_{g}")
                        j2 = kop * NST + n
                        E.op("dve", lambda e, ch2=ch, ko=ko, gp=gp, j2=j2: e.tensor_tensor(
                            out=db[gp][:], in0=dus(ch2, ko), in1=pb[j2 % 3][:], op=ALU.mult),
                            waits=[f"brep_{ch}_{kop}_{n}", f"du_{ch}_{ko}"],
                            label=f"dbu_{g}")
                        if (g - G0) >= 1:
                            emit_hcmul(E, g - 1)
                        if kop == 1 and n == 2 and koh == 0:
                            for ko2 in range(2):
                                E.op("dve", lambda e, ch2=ch, ko2=ko2: e.tensor_tensor(
                                    out=dls(ch2, ko2), in0=(pg if ko2 == 0 else pk)[:],
                                    in1=ggs(ch2, ko2), op=ALU.mult),
                                    waits=[f"pacc_{G0 + 30 + ko2}", f"silu_{ch}_{ko2}"],
                                    label=f"sg_{ch}_{ko2}")
                            for ko2 in range(2, 4):
                                E.op("pe", lambda e, ch2=ch, ko2=ko2: e.matmul(
                                    out=(pg if ko2 % 2 == 0 else pk)[:],
                                    lhsT=eyed[:, ko2 * 128:(ko2 + 1) * 128],
                                    rhs=uos(ch2, ko2), start=True, stop=False),
                                    waits=[f"uev_{ch}_{ko2}", f"sg_{ch}_{ko2 - 2}"],
                                    label=f"yinit_{ch}_{ko2}")
                        if (g - G0) >= 4:
                            emit_pacc(E, g - 4)
                        sw = [f"dA_{g}"]
                        sw += [f"st_{g - NHH}"] if g >= NHH else ["st0"]
                        E.op("dve", lambda e, gp=gp, hho=hho, scol=scol: e.tensor_tensor_scan(
                            out=hho, data0=da[gp][:], data1=db[gp][:], initial=scol,
                            op0=ALU.mult, op1=ALU.add),
                            waits=sw, label=f"scan_{g}")
                        if (g - G0) >= 2:
                            emit_act_st(E, g - 2)
                    slot = kop * NST + n
                    for sl, fn in slotted:
                        if sl == slot:
                            fn(E)
            E.op("dve", lambda e: e.drain(), label=f"scanfin_{ch}")
            emit_hcmul(E, G0 + GPC - 1)
            for gt in range(G0 + GPC - 4, G0 + GPC):
                emit_pacc(E, gt)
            emit_act_st(E, G0 + GPC - 2)
            emit_act_st(E, G0 + GPC - 1)

        def so_out(E, ch):
            p = ch % 2
            G0 = ch * GPC
            for ko in range(2, KO):
                E.op("dve", lambda e, ch2=ch, ko=ko: e.tensor_tensor(
                    out=dls(ch2, ko), in0=(pg if ko % 2 == 0 else pk)[:],
                    in1=ggs(ch2, ko), op=ALU.mult),
                    waits=[f"pacc_{G0 + 62 + ko % 2}", f"silu_{ch}_{ko}"],
                    label=f"sg_{ch}_{ko}")
            for s in range(TSUB):
                p2 = s % 2
                tw = [f"sg_{ch}_3", "ld_fc"]
                if s >= 2:
                    tw += [f"outadd_{ch}_{s - 2}"]
                elif ch + 1 < nch:
                    tw += [f"abs_{ch + 1}_{2 + p2}", f"relu_{ch + 1}_{2 + p2}"]
                for ko in range(KO):
                    E.op("pe", lambda e, ch2=ch, s=s, ko=ko, p2=p2: e.transpose(
                        out=ptr[p2][:, ko * 128:(ko + 1) * 128],
                        in_=dl[ch2 % 2][:, ko * TC + s * 128: ko * TC + (s + 1) * 128],
                        identity=eye[:]),
                        waits=tw if ko == 0 else [],
                        label=f"sgT_{ch}_{s}" if ko == 3 else None)
                ow = [f"sgT_{ch}_{s}", f"sum_{ch}"]
                E.op("dve", lambda e, s=s, p2=p2, p=p: e.tensor_tensor(
                    out=du[p][:, s * DOWN:(s + 1) * DOWN], in0=ptr[p2][:],
                    in1=xs[p][:, s * D:s * D + DOWN], op=ALU.add),
                    waits=ow, label=f"outadd_{ch}_{s}")
            E.op("sync", lambda e, p=p: e.dma_start(
                out=Od[ch * TC:(ch + 1) * TC, :].rearrange("(s p) d -> p s d", p=128),
                in_=du[p][:].rearrange("p (s d) -> p s d", d=DOWN)),
                waits=[f"outadd_{ch}_3"], label=f"odma_{ch}", sem="oq", dinc=16)

        def prog(E):
            trcol = colsb[:, 12:20]
            E.op("sync", lambda e: e.dma_start(out=wsb[:], in_=Wd),
                 label="ld_w", sem="wq", dinc=16)
            E.op("sync", lambda e: e.dma_start(out=xs[1][:, 0:KF * 96], in_=XPWd),
                 label="ld_xpw", sem="wq", dinc=16)
            E.op("sync", lambda e: e.dma_start(
                out=xs[1][0:64, KF * 96:KF * 96 + DOWN], in_=DTWd),
                label="ld_dtw", sem="wq", dinc=16)
            for nm, dst, src in (("ld_alog", alog[:], ALOGd), ("ld_cols", colsb[:], COLSd),
                                 ("ld_ws", wssb[:], WSd), ("ld_sel", id32[:], ID32d),
                                 ("ld_eye", eye[:], EYEd), ("ld_eyeb", eyeb[:], EYEBd),
                                 ("ld_ones", ones1[:], ONESd),
                                 ("ld_fc", fc[:], FCd)):
                E.op("sync", lambda e, d=dst, s2=src: e.dma_start(out=d, in_=s2),
                     label=nm, sem="wq", dinc=16)

            E.op("dve", lambda e: e.tensor_copy(
                out=dtwb[:], in_=xs[1][0:64, KF * 96:KF * 96 + DOWN]),
                waits=["ld_fc"], label="dtwcast")
            E.op("act", lambda e: e.activation(out=prec[:, 0:8], in_=trcol,
                                               func=AF.Exp, scale=-1.0),
                 waits=["ld_fc"], label="etr", fence=True)
            E.op("dve", lambda e: e.tensor_scalar(out=prec[:, 8:16], in0=prec[:, 0:8],
                 scalar1=1.0, scalar2=None, op0=ALU.add), waits=["etr"], fence=True)
            E.op("dve", lambda e: e.reciprocal(out=sigtr[:], in_=prec[:, 8:16]),
                 label="sigtr", fence=True)
            for kf in range(KF):
                E.op("dve", lambda e, k=kf: e.tensor_scalar(
                    out=xpwb[:, k * 96:(k + 1) * 96], in0=xs[1][:, k * 96:(k + 1) * 96],
                    scalar1=sigtr[:, k:k + 1], scalar2=None, op0=ALU.mult),
                    waits=["ld_fc", "sigtr"], label=f"xpwb_{kf}",
                    fence=(kf == KF - 1))
            E.op("act", lambda e: e.activation(out=spa[:, 0:KO * NST], in_=alog[:], func=AF.Exp),
                 waits=["ld_fc"], label="aexpl", fence=True)
            E.op("dve", lambda e: e.tensor_scalar(out=acoef[:], in0=spa[:, 0:KO * NST],
                 scalar1=-1.0, scalar2=None, op0=ALU.mult),
                 waits=["aexpl"], label="acoef", fence=True)
            E.op("pe", lambda e: e.matmul(out=ptr[0][:, 0:1], lhsT=ones1[:], rhs=wssb[:],
                                          start=True, stop=True),
                 waits=["ld_fc"], label="wsbc")
            E.op("act", lambda e: e.copy(out=prec[:, 16:17], in_=ptr[0][:, 0:1]),
                 waits=["wsbc"], label="wscp", fence=True)
            E.op("dve", lambda e: e.tensor_scalar(out=prec[:, 17:18], in0=prec[:, 16:17],
                 scalar1=1.0 / 127.0, scalar2=None, op0=ALU.mult),
                 waits=["wscp"], label="ws127", fence=True)
            E.op("dve", lambda e: e.memset(state[:], 0.0), label="st0", fence=True)
            for ko in range(KO):
                E.op("dve", lambda e, ko=ko: e.tensor_tensor(
                    out=eyed[:, ko * 128:(ko + 1) * 128], in0=eye[:],
                    in1=colsb[:, 4 + ko:5 + ko].broadcast_to([128, 128]),
                    op=ALU.mult),
                    waits=["ld_fc"], label="eyed" if ko == KO - 1 else None,
                    fence=(ko == KO - 1))
            E.op("dve", lambda e: e.memset(prec[:, 19:20], 1.0), fence=True)
            E.op("dve", lambda e: e.tensor_scalar(
                out=prec[:, 18:19], in0=prec[:, 19:20],
                scalar1=MAGIC52, scalar2=-MAGIC52, op0=ALU.add, op1=ALU.add),
                fence=True)
            E.op("dve", lambda e: e.tensor_scalar(
                out=prec[:, 18:19], in0=prec[:, 18:19],
                scalar1=MAGDIFF, scalar2=MAGIC, op0=ALU.mult, op1=ALU.add),
                fence=True)
            E.op("dve", lambda e: e.tensor_scalar(
                out=prec[:, 19:20], in0=prec[:, 18:19],
                scalar1=-1.0, scalar2=None, op0=ALU.mult), label="mgc", fence=True)

            emit_xin(E, 0)
            emit_xin(E, 1)
            for _, u in td_pre_units(0):
                u(E)
            for u in td_post_units(0):
                u(E)
            td_b(E, 0)
            for ch in range(1, nch):
                slotted = td_pre_units(ch) + \
                    [(10 + (i * 19) // max(1, len(td_post_units(ch)) - 1), u)
                     for i, u in enumerate(td_post_units(ch))]
                so_scan(E, ch - 1, slotted)
                so_out(E, ch - 1)
                if ch + 1 < nch:
                    emit_xin(E, ch + 1)
                td_b(E, ch)
            so_scan(E, nch - 1, [])
            so_out(E, nch - 1)

            E.wait("sync", f"odma_{nch - 1}")

        ms = {}
        prog(Em(None, sems, ms))

        block = st.enter_context(nc.Block())

        def run_pass(name):
            def body(eng):
                prog(Em(name, sems, ms, eng=eng))
            return body

        block.sync(run_pass("sync"))
        block.tensor(run_pass("pe"))
        block.scalar(run_pass("act"))
        block.vector(run_pass("dve"))
        block.gpsimd(run_pass("pool"))

    return nc


_NC_CACHE = None


def _get_nc():
    global _NC_CACHE
    if _NC_CACHE is None:
        _NC_CACHE = build_nc()
    return _NC_CACHE


_SEL_CONST = None
_EYE_CONST = None
_EYEB_CONST = None
_ONES_CONST = None


def _consts():
    global _SEL_CONST, _EYE_CONST, _EYEB_CONST, _ONES_CONST
    if _SEL_CONST is None:
        _SEL_CONST = np.eye(32, dtype=np.float32)
        _EYE_CONST = np.eye(128, dtype=np.float32)
        _EYEB_CONST = np.eye(128, dtype=np.float32).astype(ml_dtypes.bfloat16)
        _ONES_CONST = np.ones((1, 128), dtype=np.float32)
    return _SEL_CONST, _EYE_CONST, _EYEB_CONST, _ONES_CONST


def _prep_core(x_b, j, A_log, x_proj_w, dt_proj_w, dt_proj_b, D_param, trend,
               w_int8, weight_scale, bias):
    own = np.arange(j * DOWN, (j + 1) * DOWN)
    oth = np.arange((1 - j) * DOWN, (1 - j) * DOWN + DOWN)
    idx = np.concatenate([own, oth])

    x_c = np.ascontiguousarray(x_b[:, idx], dtype=np.float32)
    xpwT = np.ascontiguousarray(
        x_proj_w[:, idx].T.reshape(KF, 128, 96).transpose(1, 0, 2)).reshape(128, KF * 96).astype(np.float32)
    dtwT = np.ascontiguousarray(dt_proj_w[own, :].T, dtype=np.float32)
    wT = w_int8[own][:, idx].T.astype(np.float32)
    wT = np.ascontiguousarray(
        wT.reshape(KF, 128, KO, 128).transpose(1, 0, 2, 3)).reshape(
        128, KF * KO * 128).astype(ml_dtypes.bfloat16)
    alog_c = np.ascontiguousarray(
        A_log[own].reshape(KO, 128, NST).transpose(1, 0, 2)).reshape(128, KO * NST).astype(np.float32)
    cols = np.zeros((128, 20), dtype=np.float32)
    cols[:, 0:4] = dt_proj_b[own].reshape(KO, 128).T
    cols[:, 4:8] = D_param[own].reshape(KO, 128).T
    cols[:, 8:12] = bias[own].reshape(KO, 128).T
    cols[:, 12:20] = trend[idx].reshape(KF, 128).T
    ws_c = np.asarray(weight_scale, dtype=np.float32).reshape(1, 1)
    sel, eye_f, eye_b, ones_r = _consts()
    fcv = np.tile(np.array([[1e-5, 1e-6, -0.5, 0.0]], dtype=np.float32), (128, 1))
    return dict(x=x_c, wt=wT, xpwt=xpwT, dtwt=dtwT, alog=alog_c, cols=cols,
                wsc=ws_c, id32=sel, eye=eye_f, eyeb=eye_b, ones1=ones_r,
                fconst=fcv)


def kernel(L0_attn_x, L0_attn_A_log, L0_attn_x_proj_w, L0_attn_dt_proj_w,
           L0_attn_dt_proj_b, L0_attn_D_param, L0_attn_trend_emphasis,
           L0_ffn_w_int8, L0_ffn_weight_scale, L0_ffn_bias):
    x = np.asarray(L0_attn_x, dtype=np.float32)
    A_log = np.asarray(L0_attn_A_log, dtype=np.float32)
    xpw = np.asarray(L0_attn_x_proj_w, dtype=np.float32)
    dtw = np.asarray(L0_attn_dt_proj_w, dtype=np.float32)
    dtb = np.asarray(L0_attn_dt_proj_b, dtype=np.float32)
    Dp = np.asarray(L0_attn_D_param, dtype=np.float32)
    tr = np.asarray(L0_attn_trend_emphasis, dtype=np.float32)
    w8 = np.asarray(L0_ffn_w_int8)
    ws = np.asarray(L0_ffn_weight_scale, dtype=np.float32)
    bi = np.asarray(L0_ffn_bias, dtype=np.float32)

    nc = _get_nc()
    in_maps = []
    for c in range(8):
        b, j = c // 2, c % 2
        in_maps.append(_prep_core(x[b], j, A_log, xpw, dtw, dtb, Dp, tr, w8, ws, bi))
    res = run_bass_kernel_spmd(nc, in_maps, core_ids=list(range(8)))
    out = np.empty((4, T, D), dtype=res.results[0]["o"].dtype)
    for c in range(8):
        b, j = c // 2, c % 2
        out[b][:, j * DOWN:(j + 1) * DOWN] = res.results[c]["o"]
    return out


# version 35 (final)
# speedup vs baseline: 1.0112x; 1.0038x over previous
import numpy as np
import ml_dtypes
import concourse.bass as bass
import concourse.mybir as mybir
from concourse.bass_utils import run_bass_kernel_spmd
from contextlib import ExitStack

F32 = mybir.dt.float32
BF16 = mybir.dt.bfloat16
AF = mybir.ActivationFunctionType
ALU = mybir.AluOpType
AX = mybir.AxisListType

T = 4096
D = 1024
DOWN = 512
NST = 16
TC = 512
TSUB = TC // 128
KF = D // 128
KO = DOWN // 128
NCH = T // TC
NHH = 5
GPC = KO * NST
MAGIC = float(1.5 * 2 ** 23)
MAGIC52 = float(1.5 * 2 ** 52)
MAGDIFF = float(1.5 * 2 ** 52 - 1.5 * 2 ** 23)

(CSUM, CSQ, CM, CMSQ, CMM, CVAR, CVEP, CRVP, CVR, CRSTD, CPR, CMX, CAM,
 CAMC, CREC, CS127, CT1, CQQ, CT2, CBB2, CKFN, CLNT, CLN2,
 CMR) = [4 * i for i in range(24)]


class Em:

    def __init__(self, name, sems, ms, eng=None):
        self.name = name
        self.sems = sems
        self.ms = ms
        self.eng = eng
        self.cnt = {k: 0 for k in sems}
        self.lastw = {}

    def op(self, queue, fn, waits=(), label=None, sem=None, dinc=1, fence=False):
        sem = sem or queue
        if self.name is None:
            for w in waits:
                if isinstance(w, tuple):
                    continue
                assert w.startswith("pcfree_") or w in self.ms, \
                    f"wait on undefined label {w}"
            if label is not None:
                self.cnt[sem] += dinc
                assert label not in self.ms, f"dup label {label}"
                self.ms[label] = (sem, self.cnt[sem])
            return
        if queue != self.name:
            return
        for w in waits:
            force = False
            if isinstance(w, tuple):
                w, m2 = w
                force = (m2 is True)
            sname, val = self.ms[w]
            if sname == self.name and not force:
                continue
            if self.lastw.get(sname, -1) >= val:
                continue
            self.eng.wait_ge(self.sems[sname], val)
            self.lastw[sname] = val
        inst = fn(self.eng)
        if fence:
            inst = self.eng.drain()
        if label is not None:
            sname, _ = self.ms[label]
            inst.then_inc(self.sems[sname], dinc)

    def wait(self, queue, label):
        if self.name is None:
            assert label in self.ms, label
            return
        if queue != self.name:
            return
        sname, val = self.ms[label]
        if self.lastw.get(sname, -1) >= val:
            return
        self.eng.wait_ge(self.sems[sname], val)
        self.lastw[sname] = val


def build_nc(nch=NCH):
    nc = bass.Bass("TRN2", target_bir_lowering=False, debug=False,
                   detect_race_conditions=False)

    Xd = nc.dram_tensor("x", [T, D], F32, kind="ExternalInput").ap()
    Wd = nc.dram_tensor("wt", [128, KF * KO * 128], BF16, kind="ExternalInput").ap()
    XPWd = nc.dram_tensor("xpwt", [128, KF * 96], F32, kind="ExternalInput").ap()
    DTWd = nc.dram_tensor("dtwt", [64, DOWN], F32, kind="ExternalInput").ap()
    ALOGd = nc.dram_tensor("alog", [128, KO * NST], F32, kind="ExternalInput").ap()
    COLSd = nc.dram_tensor("cols", [128, 20], F32, kind="ExternalInput").ap()
    WSd = nc.dram_tensor("wsc", [1, 1], F32, kind="ExternalInput").ap()
    ID32d = nc.dram_tensor("id32", [32, 32], F32, kind="ExternalInput").ap()
    EYEd = nc.dram_tensor("eye", [128, 128], F32, kind="ExternalInput").ap()
    EYEBd = nc.dram_tensor("eyeb", [128, 128], BF16, kind="ExternalInput").ap()
    ONESd = nc.dram_tensor("ones1", [1, 128], F32, kind="ExternalInput").ap()
    FCd = nc.dram_tensor("fconst", [128, 4], F32, kind="ExternalInput").ap()
    CSTGd = nc.dram_tensor("cstg", [2, NST, TC], BF16, kind="Internal").ap()
    Od = nc.dram_tensor("o", [T, DOWN], F32, kind="ExternalOutput").ap()

    with ExitStack() as st:
        def sb(name, shape, dt=F32):
            return st.enter_context(nc.sbuf_tensor(name, shape, dt))

        def ps(name, shape, dt=F32):
            return st.enter_context(nc.psum_tensor(name, shape, dt))

        xs = [sb("xs0", [128, TSUB * D]), sb("xs1", [128, TSUB * D])]
        xn = sb("xn", [128, TSUB * D])
        qt = sb("qt", [128, TSUB * D], BF16)
        xnf = sb("xnf", [128, KF * TC], BF16)
        qf = sb("qf", [128, KF * TC], BF16)
        uo = [sb("uo0", [128, KO * TC]), sb("uo1", [128, KO * TC])]
        dl = [sb("dl0", [128, KO * TC]), sb("dl1", [128, KO * TC])]
        du = [sb("du0", [128, KO * TC]), sb("du1", [128, KO * TC])]
        gg = [sb("gg0", [128, KO * TC]), sb("gg1", [128, KO * TC])]
        BCs = [sb("BCs0", [32, TC]), sb("BCs1", [32, TC])]
        sgb = sb("sgb", [128, KO * TC])
        dpbf = sb("dpbf", [64, TC], BF16)
        da = [sb("da0", [128, TC]), sb("da1", [128, TC]), sb("da2", [128, TC])]
        db = [sb("db0", [128, TC]), sb("db1", [128, TC]), sb("db2", [128, TC])]
        hh = sb("hh", [128, NHH * TC])
        hcD = sb("hcD", [128, 6 * TC], BF16)
        eyed = sb("eyed", [128, KO * 128])
        BCsb = [sb("bcsb0", [32, TC], BF16), sb("bcsb1", [32, TC], BF16)]
        cring = sb("cring", [128, 4 * TC], BF16)
        spa = sb("spa", [128, TC])
        spb = sb("spb", [128, TC])
        wsb = sb("wsb", [128, KF * KO * 128], BF16)
        xpwb = sb("xpwb", [128, KF * 96], BF16)
        dtwb = sb("dtwb", [64, DOWN], BF16)
        alog = sb("alogs", [128, KO * NST])
        acoef = sb("acoefs", [128, KO * NST])
        colsb = sb("colsb", [128, 20])
        stats = sb("stats", [128, 96])
        prec = sb("prec", [128, 24])
        sigtr = sb("sigtrs", [128, KF])
        state = sb("state", [128, KO * NST])
        id32 = sb("id32s", [32, 32])
        eye = sb("eyes", [128, 128])
        eyeb = sb("eyebs", [128, 128], BF16)
        ones1 = sb("ones1s", [1, 128])
        wssb = sb("wssb", [1, 1])
        krsb = sb("krsb", [128, TC])
        fc = sb("fcs", [128, 4])
        nc.const_aps.register(F32, 1e-5, fc[:, 0:1])
        nc.const_aps.register(F32, 1e-6, fc[:, 1:2])
        nc.const_aps.register(F32, -0.5, fc[:, 2:3])

        ptr = [ps("ptr0", [128, TC]), ps("ptr1", [128, TC])]
        pq = ps("pq", [128, 2 * TC], BF16)
        pg = ps("pg", [128, TC])
        pb = [ps("pb0", [128, TC]), ps("pb1", [128, TC]), ps("pb2", [128, TC])]
        pk = ps("pk", [128, TC])

        sems = {}
        for s_ in ("pe", "act", "dve", "pool", "xin0", "xin1", "oq", "wq",
                   "cst", "cq"):
            sems[s_] = st.enter_context(nc.semaphore("sem_" + s_))

        def dls(ch, ko):
            return dl[ch % 2][:, ko * TC:(ko + 1) * TC]

        def dus(ch, ko):
            return du[ch % 2][:, ko * TC:(ko + 1) * TC]

        def uos(ch, ko):
            return uo[ch % 2][:, ko * TC:(ko + 1) * TC]

        def ggs(ch, ko):
            return gg[ch % 2][:, ko * TC:(ko + 1) * TC]

        def sgbs(ko):
            return sgb[:, ko * TC:(ko + 1) * TC]

        def hhs(g):
            ghp = g % NHH
            return hh[:, ghp * TC:(ghp + 1) * TC]

        def emit_hcmul(E, g):
            ch = g // GPC
            r = g - ch * GPC
            kop, n, koh = r // 32, (r % 32) // 2, r % 2
            hho = hhs(g)
            hcd = hcD[:, (g % 6) * TC:(g % 6 + 1) * TC]
            hw = [f"cbc_{ch}_{kop}_{n}"]
            if g >= 6:
                hw += [f"pacc_{g - 6}"]
            E.op("dve", lambda e, hho=hho, hcd=hcd, n=n: e.tensor_tensor(
                out=hcd, in0=hho, in1=cring[:, (n % 4) * TC:(n % 4 + 1) * TC],
                op=ALU.mult),
                waits=hw, label=f"hc_{g}")

        def emit_pacc(E, g):
            r = g - (g // GPC) * GPC
            kop, n, koh = r // 32, (r % 32) // 2, r % 2
            hcd = hcD[:, (g % 6) * TC:(g % 6 + 1) * TC]
            bank = pg if koh == 0 else pk
            E.op("pe", lambda e, hcd=hcd, bank=bank: e.matmul(
                out=bank[:], lhsT=eyeb[:], rhs=hcd,
                start=False, stop=(n == NST - 1)),
                waits=[f"hc_{g}"], label=f"pacc_{g}")

        def emit_act_st(E, g):
            ch = g // GPC
            r = g - ch * GPC
            kop, n, koh = r // 32, (r % 32) // 2, r % 2
            ko = kop * 2 + koh
            hho = hhs(g)
            scol = state[:, ko * NST + n:ko * NST + n + 1]
            stw = [f"scan_{g}", "st0"]
            if (g - ch * GPC) < GPC - 1:
                stw += [f"dbu_{g + 1}"]
            else:
                stw += [f"scanfin_{ch}"]
            E.op("act", lambda e, hho=hho, scol=scol: e.copy(
                out=scol, in_=hho[:, TC - 1:TC]),
                waits=stw, label=f"st_{g}",
                fence=((g - ch * GPC) == GPC - 1))

        def td_pre_units(ch):
            p = ch % 2
            xsc = xs[p]

            def xss(s):
                return xsc[:, s * D:(s + 1) * D]

            def cw4(c):
                return stats[:, c:c + 4]

            def scol4(c, s):
                return stats[:, c + s:c + s + 1]

            def u_acc1(E, s):
                sw_ = [f"xin_{ch}"]
                if s == 0:
                    sw_ += [f"var_{ch - 1}"] if ch > 0 else []
                    sw_ += [f"odma_{ch - 2}"] if ch > 1 else []
                E.op("act", lambda e, s=s: e.activation(
                    out=du[p][:, 0:D], in_=xss(s), func=AF.Copy,
                    accum_out=scol4(CSUM, s)),
                    waits=sw_, label=f"sum_{ch}" if s == 3 else None, fence=(s == 3))

            def u_acc2(E, s):
                E.op("act", lambda e, s=s: e.activation(
                    out=du[p][:, 0:D], in_=xss(s), func=AF.Square,
                    accum_out=scol4(CSQ, s)),
                    label=f"sq_{ch}_{s}", fence=(s == 3))

            def u_cols(E):
                mw = [f"sum_{ch}"] + ([f"prs_{ch - 1}"] if ch > 0 else [])
                E.op("dve", lambda e: e.tensor_scalar(out=cw4(CM), in0=cw4(CSUM),
                     scalar1=1.0 / D, scalar2=None, op0=ALU.mult), waits=mw, fence=True)
                E.op("dve", lambda e: e.tensor_scalar(out=cw4(CMSQ), in0=cw4(CSQ),
                     scalar1=1.0 / D, scalar2=None, op0=ALU.mult),
                     waits=[f"sq_{ch}_3"], fence=True)
                E.op("dve", lambda e: e.tensor_tensor(out=cw4(CMM), in0=cw4(CM),
                     in1=cw4(CM), op=ALU.mult), fence=True)
                E.op("dve", lambda e: e.tensor_tensor(out=cw4(CVAR), in0=cw4(CMSQ),
                     in1=cw4(CMM), op=ALU.subtract), fence=True)
                E.op("dve", lambda e: e.tensor_scalar(out=cw4(CVEP), in0=cw4(CVAR),
                     scalar1=1e-5, scalar2=None, op0=ALU.add), fence=True)
                E.op("dve", lambda e: e.reciprocal(out=cw4(CRVP), in_=cw4(CVEP)), fence=True)
                E.op("dve", lambda e: e.tensor_tensor(out=cw4(CVR), in0=cw4(CVAR),
                     in1=cw4(CRVP), op=ALU.mult), label=f"var_{ch}", fence=True)
                E.op("act", lambda e: e.activation(out=cw4(CLNT), in_=cw4(CVAR),
                     func=AF.Ln, bias=1e-5),
                     waits=[f"var_{ch}", "ld_fc"] + ([f"q_{ch - 1}_3"] if ch > 0 else []),
                     fence=True)
                E.op("act", lambda e: e.activation(out=cw4(CRSTD), in_=cw4(CLNT),
                     func=AF.Exp, scale=-0.5), fence=True)
                E.op("act", lambda e: e.activation(out=cw4(CLN2), in_=cw4(CVR),
                     func=AF.Ln, bias=1e-6), fence=True)
                E.op("act", lambda e: e.activation(out=cw4(CPR), in_=cw4(CLN2),
                     func=AF.Exp, scale=-0.5), label=f"prs_{ch}", fence=True)

            def u_xn(E):
                E.op("dve", lambda e: e.scalar_tensor_tensor(
                    out=cw4(CMR), in0=cw4(CM), scalar=-1.0, in1=cw4(CRSTD),
                    op0=ALU.mult, op1=ALU.mult),
                    waits=[f"prs_{ch}"], label=f"cmr_{ch}", fence=True)
                for s in range(TSUB):
                    xw = [f"cmr_{ch}"]
                    if ch > 0 and s == 0:
                        xw += [f"xnT_{ch - 1}_{KF - 1}", f"spadd_{ch - 1}_{KO - 1}"]
                    E.op("act", lambda e, s=s: e.activation(
                        out=xn[:, s * D:(s + 1) * D], in_=xss(s), func=AF.Identity,
                        scale=scol4(CRSTD, s), bias=scol4(CMR, s)),
                        waits=xw, label=f"xn_{ch}_{s}")
                    E.op("dve", lambda e, s=s: e.tensor_reduce(
                        out=scol4(CMX, s), in_=xn[:, s * D:(s + 1) * D],
                        axis=AX.X, op=ALU.max, apply_absolute_value=True),
                        waits=[f"xn_{ch}_{s}"], fence=(s == 3))
                E.op("dve", lambda e: e.tensor_tensor(out=cw4(CAM), in0=cw4(CMX),
                     in1=cw4(CPR), op=ALU.mult), fence=True)
                E.op("dve", lambda e: e.tensor_scalar(out=cw4(CAMC), in0=cw4(CAM),
                     scalar1=1e-5, scalar2=None, op0=ALU.max), fence=True)
                E.op("dve", lambda e: e.reciprocal(out=cw4(CREC), in_=cw4(CAMC)), fence=True)
                E.op("dve", lambda e: e.tensor_scalar(out=cw4(CS127), in0=cw4(CREC),
                     scalar1=127.0, scalar2=None, op0=ALU.mult), fence=True)
                E.op("dve", lambda e: e.tensor_tensor(out=cw4(CT1), in0=cw4(CRSTD),
                     in1=cw4(CPR), op=ALU.mult), fence=True)
                E.op("dve", lambda e: e.tensor_tensor(out=cw4(CQQ), in0=cw4(CT1),
                     in1=cw4(CS127), op=ALU.mult), fence=True)
                E.op("dve", lambda e: e.tensor_tensor(out=cw4(CT2), in0=cw4(CM),
                     in1=cw4(CQQ), op=ALU.mult), fence=True)
                E.op("dve", lambda e: e.tensor_scalar(out=cw4(CBB2), in0=cw4(CT2),
                     scalar1=-1.0, scalar2=None, op0=ALU.mult), fence=True)
                E.op("dve", lambda e: e.tensor_tensor(
                    out=cw4(CKFN), in0=cw4(CAMC),
                    in1=prec[:, 17:18].broadcast_to([128, 4]), op=ALU.mult),
                    waits=["ws127"] + ([f"kcT_{ch - 1}"] if ch > 0 else []),
                    label=f"kfn_{ch}", fence=True)

            def u_q(E):
                for s in range(TSUB):
                    for hlf in range(2):
                        h8 = s * 2 + hlf
                        off = s * D + hlf * TC
                        qw = [f"kfn_{ch}"]
                        if h8 == 0 and ch > 0:
                            qw += [f"qT_{ch - 1}_{KF - 1}", f"abs_{ch - 1}_{KO - 1}",
                                   f"lnl_{ch - 1}_{KO - 1}"]
                        if h8 >= 2:
                            qw += [f"qm_{ch}_{h8 - 2}"]
                        E.op("act", lambda e, off=off, s=s, h8=h8: e.activation(
                            out=spa[:, 0:TC] if h8 % 2 == 0 else spb[:, 0:TC],
                            in_=xsc[:, off:off + TC], func=AF.Identity,
                            scale=scol4(CQQ, s), bias=scol4(CBB2, s)),
                            waits=qw, label=f"qs_{ch}_{h8}")
                        E.op("dve", lambda e, off=off, h8=h8: e.tensor_scalar(
                            out=qt[:, off:off + TC],
                            in0=spa[:, 0:TC] if h8 % 2 == 0 else spb[:, 0:TC],
                            scalar1=prec[:, 18:19], scalar2=prec[:, 19:20],
                            op0=ALU.add, op1=ALU.add),
                            waits=[f"qs_{ch}_{h8}"],
                            label=f"q_{ch}_{s}" if (s == 3 and hlf == 1) else f"qm_{ch}_{h8}")

            return ([(1 + s, lambda E, s=s: u_acc1(E, s)) for s in range(TSUB)] +
                    [(5 + s // 2, lambda E, s=s: u_acc2(E, s)) for s in range(TSUB)] +
                    [(7, u_cols), (8, u_xn), (9, u_q)])

        def td_post_units(ch):
            p = ch % 2

            def scol4(c, s2):
                return stats[:, c + s2:c + s2 + 1]

            units = []

            def u_xnt(E, kf):
                p2 = kf % 2
                tw = [f"xn_{ch}_3", "ld_fc"]
                if ch == 0 and kf == 0:
                    tw += ["wscp"]
                if kf >= 2:
                    tw += [f"xnbfev_{ch}_{kf - 2}"]
                    if kf - 2 < KO:
                        tw += [f"uev_{ch}_{kf - 2}"]
                elif ch >= 2:
                    tw += [f"outadd_{ch - 2}_3"]
                if kf < 2 and ch >= 1:
                    tw += [f"abs_{ch - 1}_{2 + p2}", f"relu_{ch - 1}_{2 + p2}"]
                for s in range(TSUB):
                    E.op("pe", lambda e, kf=kf, s=s, p2=p2: e.transpose(
                        out=ptr[p2][:, s * 128:(s + 1) * 128],
                        in_=xn[:, s * D + kf * 128: s * D + (kf + 1) * 128],
                        identity=eye[:]),
                        waits=tw if s == 0 else [],
                        label=f"xnT_{ch}_{kf}" if s == 3 else None)
                if kf < KO:
                    E.op("act", lambda e, kf=kf, p2=p2: e.mul(
                        out=uos(ch, kf), in_=ptr[p2][:], mul=sigtr[:, kf:kf + 1]),
                        waits=[f"xnT_{ch}_{kf}", "sigtr"] +
                              ([f"yinit_{ch - 2}_{kf}", f"du_{ch - 2}_{kf}"] if ch > 1 else []),
                        label=f"uev_{ch}_{kf}")
                E.op("act", lambda e, kf=kf, p2=p2: e.copy(
                    out=xnf[:, kf * TC:(kf + 1) * TC], in_=ptr[p2][:]),
                    waits=[f"xnT_{ch}_{kf}"] + ([f"xz_{ch - 1}"] if ch > 0 and kf < 2 else []),
                    label=f"xnbfev_{ch}_{kf}")

            for kf in range(KF):
                units.append(lambda E, kf=kf: u_xnt(E, kf))

            def u_qt(E, kf):
                qw = [f"q_{ch}_3", "ld_fc"]
                if kf >= 1:
                    qw += [f"qfev_{ch}_{kf - 1}"]
                elif ch > 0:
                    qw += [f"qfev_{ch - 1}_{KF - 1}"]
                for s in range(TSUB):
                    E.op("pe", lambda e, kf=kf, s=s: e.transpose(
                        out=pq[:, s * 128:(s + 1) * 128],
                        in_=qt[:, s * D + kf * 128: s * D + (kf + 1) * 128],
                        identity=eyeb[:]),
                        waits=qw if s == 0 else [],
                        label=f"qT_{ch}_{kf}" if s == 3 else None)
                E.op("act", lambda e, kf=kf: e.copy(
                    out=qf[:, kf * TC:(kf + 1) * TC], in_=pq[:, 0:TC]),
                    waits=[f"qT_{ch}_{kf}"] + ([f"ffn_{ch - 1}_3"] if ch > 0 and kf == 0 else []),
                    label=f"qfev_{ch}_{kf}")

            for kf in range(KF):
                units.append(lambda E, kf=kf: u_qt(E, kf))

            def u_xz(E, half):
                for kf in range(half * 4, half * 4 + 4):
                    zw = [f"xnbfev_{ch}_{kf}", f"xpwb_{kf}"]
                    if kf == 0:
                        zw += [f"xnbfev_{ch}_{KF - 2}"]
                    E.op("pe", lambda e, kf=kf: e.matmul(
                        out=ptr[0][0:96, :], lhsT=xpwb[:, kf * 96:(kf + 1) * 96],
                        rhs=xnf[:, kf * TC:(kf + 1) * TC],
                        start=(kf == 0), stop=(kf == KF - 1)),
                        waits=zw, label=f"xz_{ch}" if kf == KF - 1 else None)
                if half == 0:
                    return
                dpw = [f"xz_{ch}"]
                if ch > 0:
                    dpw += [f"dmm_{ch - 1}_3"]
                E.op("act", lambda e: e.copy(out=dpbf[:], in_=ptr[0][0:64, :]),
                     waits=dpw, label=f"dpbfev_{ch}")
                bw = [f"xz_{ch}"]
                if ch >= 2:
                    bw += [f"brep_{ch - 2}_1_15"]
                E.op("act", lambda e: e.copy(out=BCs[p][:], in_=ptr[0][64:96, :]),
                     waits=bw, label=f"bcev_{ch}")
                bw2 = [f"cstg_{ch - 2}"] if ch >= 2 else []
                E.op("act", lambda e: e.copy(out=BCsb[p][:], in_=ptr[0][64:96, :]),
                     waits=bw2, label=f"bcevb_{ch}")
                E.op("sync", lambda e: e.dma_start(
                    out=CSTGd[p], in_=BCsb[p][16:32, :]),
                    waits=[f"bcevb_{ch}"], label=f"cstg_{ch}", sem="cst", dinc=16)

            units.append(lambda E: u_xz(E, 0))
            units.append(lambda E: u_xz(E, 1))

            def u_dmm(E, ko):
                p2 = ko % 2
                dw = [f"dpbfev_{ch}", "dtwcast"]
                if ko < 2:
                    if p2 == 0:
                        dw += [f"bcevb_{ch}"]
                    else:
                        dw += [f"xnbfev_{ch}_{KF - 1}"]
                else:
                    dw += [f"abs_{ch}_{ko - 2}", f"relu_{ch}_{ko - 2}"]
                E.op("pe", lambda e, ko=ko, p2=p2: e.matmul(
                    out=ptr[p2][:], lhsT=dtwb[:, ko * 128:(ko + 1) * 128],
                    rhs=dpbf[:], start=True, stop=True),
                    waits=dw, label=f"dmm_{ch}_{ko}")
                dtb = colsb[:, ko:ko + 1]
                aw = [f"dmm_{ch}_{ko}", "ld_fc"]
                if ko == 0:
                    aw += [f"q_{ch}_3"]
                E.op("act", lambda e, p2=p2, dtb=dtb: e.activation(
                    out=spa[:], in_=ptr[p2][:], func=AF.Abs, bias=dtb),
                    waits=aw, label=f"abs_{ch}_{ko}")
                E.op("act", lambda e: e.activation(out=spb[:], in_=spa[:],
                     func=AF.Exp, scale=-1.0))

            def u_dmm2(E, ko):
                p2 = ko % 2
                dtb = colsb[:, ko:ko + 1]
                E.op("act", lambda e, ko=ko: e.activation(
                    out=xn[:, ko * TC:(ko + 1) * TC], in_=spb[:],
                    func=AF.Ln, bias=1.0), label=f"lnl_{ch}_{ko}")
                rw = [f"dmm_{ch}_{ko}", "ld_fc", f"abs_{ch}_{ko}"]
                if ch >= 2:
                    rw += [f"dA_{(ch - 2) * GPC + (ko // 2) * 32 + 30 + ko % 2}",
                           f"sgT_{ch - 2}_3"]
                E.op("act", lambda e, ch2=ch, ko=ko, p2=p2, dtb=dtb: e.activation(
                    out=dls(ch2, ko), in_=ptr[p2][:], func=AF.Relu, bias=dtb),
                    waits=rw, label=f"relu_{ch}_{ko}")

            for ko in range(KO):
                units.append(lambda E, ko=ko: u_dmm(E, ko))
                units.append(lambda E, ko=ko: u_dmm2(E, ko))

            return units

        def td_b(E, ch):
            def scol4(c, s2):
                return stats[:, c + s2:c + s2 + 1]

            for s2 in range(TSUB):
                kw = [f"kfn_{ch}", "ld_fc"]
                if s2 == 0:
                    if ch > 0:
                        kw += [f"krsbev_{ch - 1}", f"sg_{ch - 1}_3"]
                E.op("pe", lambda e, s2=s2: e.transpose(
                    out=pk[0:1, s2 * 128:(s2 + 1) * 128],
                    in_=scol4(CKFN, s2), identity=eye[:]),
                    waits=kw, label=f"kcT_{ch}" if s2 == 3 else None)
            E.op("act", lambda e: e.copy(out=spb[0:1, 0:TC], in_=pk[0:1, :]),
                 waits=[f"kcT_{ch}"], label=f"krowev_{ch}")
            E.op("pe", lambda e: e.matmul(out=pk[:], lhsT=ones1[:], rhs=spb[0:1, 0:TC],
                                          start=True, stop=True),
                 waits=[f"krowev_{ch}", "ld_fc"], label=f"k1_{ch}")
            E.op("act", lambda e: e.copy(out=krsb[:], in_=pk[:]),
                 waits=[f"k1_{ch}"], label=f"krsbev_{ch}")
            for ko in range(KO):
                E.op("dve", lambda e, ch2=ch, ko=ko: e.tensor_tensor(
                    out=dls(ch2, ko), in0=dls(ch2, ko),
                    in1=xn[:, ko * TC:(ko + 1) * TC], op=ALU.add),
                    waits=[f"lnl_{ch}_{ko}", f"relu_{ch}_{ko}"], label=f"spadd_{ch}_{ko}")
                duw = [f"spadd_{ch}_{ko}", f"uev_{ch}_{ko}"]
                if ch >= 2:
                    duw += [f"dbu_{(ch - 2) * GPC + (ko // 2) * 32 + 30 + ko % 2}",
                            f"odma_{ch - 2}"]
                E.op("dve", lambda e, ch2=ch, ko=ko: e.tensor_tensor(
                    out=dus(ch2, ko), in0=dls(ch2, ko), in1=uos(ch2, ko), op=ALU.mult),
                    waits=duw, label=f"du_{ch}_{ko}")
            for o in range(KO):
                fw = [f"qfev_{ch}_{KF - 1}", "ld_fc"]
                if o > 0:
                    fw += [f"ffnz_{ch}_{o - 1}"]
                elif ch > 0:
                    fw += [f"sg_{ch - 1}_2"]
                for kf in range(KF):
                    E.op("pe", lambda e, kf=kf, o=o: e.matmul(
                        out=pg[:], lhsT=wsb[:, (kf * KO + o) * 128:(kf * KO + o + 1) * 128],
                        rhs=qf[:, kf * TC:(kf + 1) * TC],
                        start=(kf == 0), stop=(kf == KF - 1)),
                        waits=fw if kf == 0 else [],
                        label=f"ffn_{ch}_{o}" if kf == KF - 1 else None)
                zw = [f"ffn_{ch}_{o}", f"krsbev_{ch}"]
                if ch > 0:
                    zw += [f"silu_{ch - 1}_{o}"]
                E.op("dve", lambda e, o=o: e.tensor_tensor(
                    out=sgbs(o), in0=pg[:], in1=krsb[:], op=ALU.mult),
                    waits=zw, label=f"ffnz_{ch}_{o}")
                sw = [f"ffnz_{ch}_{o}", "ld_fc"]
                if ch >= 2:
                    sw += [f"sg_{ch - 2}_{o}"]
                E.op("act", lambda e, ch2=ch, o=o: e.activation(
                    out=ggs(ch2, o), in_=sgbs(o), func=AF.Silu,
                    bias=colsb[:, 8 + o:9 + o]),
                    waits=sw, label=f"silu_{ch}_{o}")

        def emit_xin(E, ch):
            p = ch % 2
            if ch == 0:
                xw = []
            elif ch == 1:
                xw = [f"xpwb_{KF - 1}", "dtwcast"]
            else:
                xw = [f"outadd_{ch - 2}_3", f"sum_{ch - 2}"]
            E.op("sync", lambda e, p=p: e.dma_start(
                out=xs[p][:].rearrange("p (s d) -> p s d", d=D),
                in_=Xd[ch * TC:(ch + 1) * TC, :].rearrange("(s p) d -> p s d", p=128)),
                waits=xw, label=f"xin_{ch}", sem=f"xin{p}", dinc=16)

        def so_scan(E, ch, slotted):
            p = ch % 2
            G0 = ch * GPC
            for ko in range(2):
                yw = [f"uev_{ch}_{ko}", "ld_fc", "eyed"]
                yw += [f"ffnz_{ch}_{KO - 1}"] if ko == 0 else [f"krsbev_{ch}"]
                E.op("pe", lambda e, ch2=ch, ko=ko: e.matmul(
                    out=(pg if ko % 2 == 0 else pk)[:],
                    lhsT=eyed[:, ko * 128:(ko + 1) * 128], rhs=uos(ch2, ko),
                    start=True, stop=False),
                    waits=yw, label=f"yinit_{ch}_{ko}")
            for kop in range(2):
                for n in range(NST):
                    j = kop * NST + n
                    bw = [f"bcev_{ch}", "ld_fc"]
                    if j >= 3:
                        kp, np_ = (j - 3) // NST, (j - 3) % NST
                        bw += [f"dbu_{G0 + kp * 32 + np_ * 2 + 1}"]
                    elif ch > 0:
                        jj = 2 * NST - 3 + j
                        kp, np_ = jj // NST, jj % NST
                        bw += [f"dbu_{G0 - 64 + kp * 32 + np_ * 2 + 1}"]
                    E.op("pe", lambda e, n=n, j=j, p=p: e.matmul(
                        out=pb[j % 3][:], lhsT=id32[:, n:n + 1].broadcast_to([32, 128]),
                        rhs=BCs[p][:], start=True, stop=True),
                        waits=bw, label=f"brep_{ch}_{kop}_{n}")
                    cw2 = [f"cstg_{ch}"]
                    if n >= 4:
                        cw2 += [f"hc_{G0 + kop * 32 + (n - 4) * 2 + 1}"]
                    elif kop == 1:
                        cw2 += [f"hc_{G0 + (n + 12) * 2 + 1}"]
                    elif ch > 0:
                        cw2 += [f"hc_{G0 - 64 + 32 + (n + 12) * 2 + 1}"]
                    E.op("pool", lambda e, n=n, p=p: e.dma_start(
                        out=cring[:, (n % 4) * TC:(n % 4 + 1) * TC],
                        in_=CSTGd[p][n:n + 1, :].broadcast_to([128, TC])),
                        waits=cw2, label=f"cbc_{ch}_{kop}_{n}", sem="cq", dinc=16)
                    for koh in range(2):
                        ko = kop * 2 + koh
                        g = G0 + kop * 32 + n * 2 + koh
                        gp = g % 3
                        hho = hhs(g)
                        scol = state[:, ko * NST + n:ko * NST + n + 1]
                        aw = [f"spadd_{ch}_{ko}", "acoef"]
                        if g >= 3:
                            aw += [f"scan_{g - 3}"]
                        E.op("act", lambda e, ch2=ch, ko=ko, n=n, gp=gp: e.activation(
                            out=da[gp][:], in_=dls(ch2, ko), func=AF.Exp,
                            scale=acoef[:, ko * NST + n:ko * NST + n + 1]),
                            waits=aw, label=f"dA_{g}")
                        j2 = kop * NST + n
                        E.op("dve", lambda e, ch2=ch, ko=ko, gp=gp, j2=j2: e.tensor_tensor(
                            out=db[gp][:], in0=dus(ch2, ko), in1=pb[j2 % 3][:], op=ALU.mult),
                            waits=[f"brep_{ch}_{kop}_{n}", f"du_{ch}_{ko}"],
                            label=f"dbu_{g}")
                        if (g - G0) >= 1:
                            emit_hcmul(E, g - 1)
                        if kop == 1 and n == 2 and koh == 0:
                            for ko2 in range(2):
                                E.op("dve", lambda e, ch2=ch, ko2=ko2: e.tensor_tensor(
                                    out=dls(ch2, ko2), in0=(pg if ko2 == 0 else pk)[:],
                                    in1=ggs(ch2, ko2), op=ALU.mult),
                                    waits=[f"pacc_{G0 + 30 + ko2}", f"silu_{ch}_{ko2}"],
                                    label=f"sg_{ch}_{ko2}")
                            for ko2 in range(2, 4):
                                E.op("pe", lambda e, ch2=ch, ko2=ko2: e.matmul(
                                    out=(pg if ko2 % 2 == 0 else pk)[:],
                                    lhsT=eyed[:, ko2 * 128:(ko2 + 1) * 128],
                                    rhs=uos(ch2, ko2), start=True, stop=False),
                                    waits=[f"uev_{ch}_{ko2}", f"sg_{ch}_{ko2 - 2}"],
                                    label=f"yinit_{ch}_{ko2}")
                        if (g - G0) >= 4:
                            emit_pacc(E, g - 4)
                        sw = [f"dA_{g}"]
                        sw += [f"st_{g - NHH}"] if g >= NHH else ["st0"]
                        E.op("dve", lambda e, gp=gp, hho=hho, scol=scol: e.tensor_tensor_scan(
                            out=hho, data0=da[gp][:], data1=db[gp][:], initial=scol,
                            op0=ALU.mult, op1=ALU.add),
                            waits=sw, label=f"scan_{g}")
                        if (g - G0) >= 2:
                            emit_act_st(E, g - 2)
                    slot = kop * NST + n
                    for sl, fn in slotted:
                        if sl == slot:
                            fn(E)
            E.op("dve", lambda e: e.drain(), label=f"scanfin_{ch}")
            emit_hcmul(E, G0 + GPC - 1)
            for gt in range(G0 + GPC - 4, G0 + GPC):
                emit_pacc(E, gt)
            emit_act_st(E, G0 + GPC - 2)
            emit_act_st(E, G0 + GPC - 1)

        def so_out(E, ch):
            p = ch % 2
            G0 = ch * GPC
            for ko in range(2, KO):
                E.op("dve", lambda e, ch2=ch, ko=ko: e.tensor_tensor(
                    out=dls(ch2, ko), in0=(pg if ko % 2 == 0 else pk)[:],
                    in1=ggs(ch2, ko), op=ALU.mult),
                    waits=[f"pacc_{G0 + 62 + ko % 2}", f"silu_{ch}_{ko}"],
                    label=f"sg_{ch}_{ko}")
            for s in range(TSUB):
                p2 = s % 2
                tw = [f"sg_{ch}_3", "ld_fc"]
                if s >= 2:
                    tw += [f"outadd_{ch}_{s - 2}"]
                elif ch + 1 < nch:
                    tw += [f"abs_{ch + 1}_{2 + p2}", f"relu_{ch + 1}_{2 + p2}"]
                for ko in range(KO):
                    E.op("pe", lambda e, ch2=ch, s=s, ko=ko, p2=p2: e.transpose(
                        out=ptr[p2][:, ko * 128:(ko + 1) * 128],
                        in_=dl[ch2 % 2][:, ko * TC + s * 128: ko * TC + (s + 1) * 128],
                        identity=eye[:]),
                        waits=tw if ko == 0 else [],
                        label=f"sgT_{ch}_{s}" if ko == 3 else None)
                ow = [f"sgT_{ch}_{s}", f"sum_{ch}"]
                E.op("dve", lambda e, s=s, p2=p2, p=p: e.tensor_tensor(
                    out=du[p][:, s * DOWN:(s + 1) * DOWN], in0=ptr[p2][:],
                    in1=xs[p][:, s * D:s * D + DOWN], op=ALU.add),
                    waits=ow, label=f"outadd_{ch}_{s}")
            E.op("sync", lambda e, p=p: e.dma_start(
                out=Od[ch * TC:(ch + 1) * TC, :].rearrange("(s p) d -> p s d", p=128),
                in_=du[p][:].rearrange("p (s d) -> p s d", d=DOWN)),
                waits=[f"outadd_{ch}_3"], label=f"odma_{ch}", sem="oq", dinc=16)

        def prog(E):
            trcol = colsb[:, 12:20]
            E.op("sync", lambda e: e.dma_start(out=wsb[:], in_=Wd),
                 label="ld_w", sem="wq", dinc=16)
            E.op("sync", lambda e: e.dma_start(out=xs[1][:, 0:KF * 96], in_=XPWd),
                 label="ld_xpw", sem="wq", dinc=16)
            E.op("sync", lambda e: e.dma_start(
                out=xs[1][0:64, KF * 96:KF * 96 + DOWN], in_=DTWd),
                label="ld_dtw", sem="wq", dinc=16)
            for nm, dst, src in (("ld_alog", alog[:], ALOGd), ("ld_cols", colsb[:], COLSd),
                                 ("ld_ws", wssb[:], WSd), ("ld_sel", id32[:], ID32d),
                                 ("ld_eye", eye[:], EYEd), ("ld_eyeb", eyeb[:], EYEBd),
                                 ("ld_ones", ones1[:], ONESd),
                                 ("ld_fc", fc[:], FCd)):
                E.op("sync", lambda e, d=dst, s2=src: e.dma_start(out=d, in_=s2),
                     label=nm, sem="wq", dinc=16)

            E.op("dve", lambda e: e.tensor_copy(
                out=dtwb[:], in_=xs[1][0:64, KF * 96:KF * 96 + DOWN]),
                waits=["ld_fc"], label="dtwcast")
            E.op("act", lambda e: e.activation(out=prec[:, 0:8], in_=trcol,
                                               func=AF.Exp, scale=-1.0),
                 waits=["ld_fc"], label="etr", fence=True)
            E.op("dve", lambda e: e.tensor_scalar(out=prec[:, 8:16], in0=prec[:, 0:8],
                 scalar1=1.0, scalar2=None, op0=ALU.add), waits=["etr"], fence=True)
            E.op("dve", lambda e: e.reciprocal(out=sigtr[:], in_=prec[:, 8:16]),
                 label="sigtr", fence=True)
            for kf in range(KF):
                E.op("dve", lambda e, k=kf: e.tensor_scalar(
                    out=xpwb[:, k * 96:(k + 1) * 96], in0=xs[1][:, k * 96:(k + 1) * 96],
                    scalar1=sigtr[:, k:k + 1], scalar2=None, op0=ALU.mult),
                    waits=["ld_fc", "sigtr"], label=f"xpwb_{kf}",
                    fence=(kf == KF - 1))
            E.op("act", lambda e: e.activation(out=spa[:, 0:KO * NST], in_=alog[:], func=AF.Exp),
                 waits=["ld_fc"], label="aexpl", fence=True)
            E.op("dve", lambda e: e.tensor_scalar(out=acoef[:], in0=spa[:, 0:KO * NST],
                 scalar1=-1.0, scalar2=None, op0=ALU.mult),
                 waits=["aexpl"], label="acoef", fence=True)
            E.op("pe", lambda e: e.matmul(out=ptr[0][:, 0:1], lhsT=ones1[:], rhs=wssb[:],
                                          start=True, stop=True),
                 waits=["ld_fc"], label="wsbc")
            E.op("act", lambda e: e.copy(out=prec[:, 16:17], in_=ptr[0][:, 0:1]),
                 waits=["wsbc"], label="wscp", fence=True)
            E.op("dve", lambda e: e.tensor_scalar(out=prec[:, 17:18], in0=prec[:, 16:17],
                 scalar1=1.0 / 127.0, scalar2=None, op0=ALU.mult),
                 waits=["wscp"], label="ws127", fence=True)
            E.op("dve", lambda e: e.memset(state[:], 0.0), label="st0", fence=True)
            for ko in range(KO):
                E.op("dve", lambda e, ko=ko: e.tensor_tensor(
                    out=eyed[:, ko * 128:(ko + 1) * 128], in0=eye[:],
                    in1=colsb[:, 4 + ko:5 + ko].broadcast_to([128, 128]),
                    op=ALU.mult),
                    waits=["ld_fc"], label="eyed" if ko == KO - 1 else None,
                    fence=(ko == KO - 1))
            E.op("dve", lambda e: e.memset(prec[:, 19:20], 1.0), fence=True)
            E.op("dve", lambda e: e.tensor_scalar(
                out=prec[:, 18:19], in0=prec[:, 19:20],
                scalar1=MAGIC52, scalar2=-MAGIC52, op0=ALU.add, op1=ALU.add),
                fence=True)
            E.op("dve", lambda e: e.tensor_scalar(
                out=prec[:, 18:19], in0=prec[:, 18:19],
                scalar1=MAGDIFF, scalar2=MAGIC, op0=ALU.mult, op1=ALU.add),
                fence=True)
            E.op("dve", lambda e: e.tensor_scalar(
                out=prec[:, 19:20], in0=prec[:, 18:19],
                scalar1=-1.0, scalar2=None, op0=ALU.mult), label="mgc", fence=True)

            emit_xin(E, 0)
            emit_xin(E, 1)
            for _, u in td_pre_units(0):
                u(E)
            for u in td_post_units(0):
                u(E)
            td_b(E, 0)
            for ch in range(1, nch):
                slotted = td_pre_units(ch) + \
                    [(10 + (i * 17) // max(1, len(td_post_units(ch)) - 1), u)
                     for i, u in enumerate(td_post_units(ch))]
                so_scan(E, ch - 1, slotted)
                so_out(E, ch - 1)
                if ch + 1 < nch:
                    emit_xin(E, ch + 1)
                td_b(E, ch)
            so_scan(E, nch - 1, [])
            so_out(E, nch - 1)

            E.wait("sync", f"odma_{nch - 1}")

        ms = {}
        prog(Em(None, sems, ms))

        block = st.enter_context(nc.Block())

        def run_pass(name):
            def body(eng):
                prog(Em(name, sems, ms, eng=eng))
            return body

        block.sync(run_pass("sync"))
        block.tensor(run_pass("pe"))
        block.scalar(run_pass("act"))
        block.vector(run_pass("dve"))
        block.gpsimd(run_pass("pool"))

    return nc


_NC_CACHE = None


def _get_nc():
    global _NC_CACHE
    if _NC_CACHE is None:
        _NC_CACHE = build_nc()
    return _NC_CACHE


_SEL_CONST = None
_EYE_CONST = None
_EYEB_CONST = None
_ONES_CONST = None


def _consts():
    global _SEL_CONST, _EYE_CONST, _EYEB_CONST, _ONES_CONST
    if _SEL_CONST is None:
        _SEL_CONST = np.eye(32, dtype=np.float32)
        _EYE_CONST = np.eye(128, dtype=np.float32)
        _EYEB_CONST = np.eye(128, dtype=np.float32).astype(ml_dtypes.bfloat16)
        _ONES_CONST = np.ones((1, 128), dtype=np.float32)
    return _SEL_CONST, _EYE_CONST, _EYEB_CONST, _ONES_CONST


def _prep_core(x_b, j, A_log, x_proj_w, dt_proj_w, dt_proj_b, D_param, trend,
               w_int8, weight_scale, bias):
    own = np.arange(j * DOWN, (j + 1) * DOWN)
    oth = np.arange((1 - j) * DOWN, (1 - j) * DOWN + DOWN)
    idx = np.concatenate([own, oth])

    x_c = np.ascontiguousarray(x_b[:, idx], dtype=np.float32)
    xpwT = np.ascontiguousarray(
        x_proj_w[:, idx].T.reshape(KF, 128, 96).transpose(1, 0, 2)).reshape(128, KF * 96).astype(np.float32)
    dtwT = np.ascontiguousarray(dt_proj_w[own, :].T, dtype=np.float32)
    wT = w_int8[own][:, idx].T.astype(np.float32)
    wT = np.ascontiguousarray(
        wT.reshape(KF, 128, KO, 128).transpose(1, 0, 2, 3)).reshape(
        128, KF * KO * 128).astype(ml_dtypes.bfloat16)
    alog_c = np.ascontiguousarray(
        A_log[own].reshape(KO, 128, NST).transpose(1, 0, 2)).reshape(128, KO * NST).astype(np.float32)
    cols = np.zeros((128, 20), dtype=np.float32)
    cols[:, 0:4] = dt_proj_b[own].reshape(KO, 128).T
    cols[:, 4:8] = D_param[own].reshape(KO, 128).T
    cols[:, 8:12] = bias[own].reshape(KO, 128).T
    cols[:, 12:20] = trend[idx].reshape(KF, 128).T
    ws_c = np.asarray(weight_scale, dtype=np.float32).reshape(1, 1)
    sel, eye_f, eye_b, ones_r = _consts()
    fcv = np.tile(np.array([[1e-5, 1e-6, -0.5, 0.0]], dtype=np.float32), (128, 1))
    return dict(x=x_c, wt=wT, xpwt=xpwT, dtwt=dtwT, alog=alog_c, cols=cols,
                wsc=ws_c, id32=sel, eye=eye_f, eyeb=eye_b, ones1=ones_r,
                fconst=fcv)


def kernel(L0_attn_x, L0_attn_A_log, L0_attn_x_proj_w, L0_attn_dt_proj_w,
           L0_attn_dt_proj_b, L0_attn_D_param, L0_attn_trend_emphasis,
           L0_ffn_w_int8, L0_ffn_weight_scale, L0_ffn_bias):
    x = np.asarray(L0_attn_x, dtype=np.float32)
    A_log = np.asarray(L0_attn_A_log, dtype=np.float32)
    xpw = np.asarray(L0_attn_x_proj_w, dtype=np.float32)
    dtw = np.asarray(L0_attn_dt_proj_w, dtype=np.float32)
    dtb = np.asarray(L0_attn_dt_proj_b, dtype=np.float32)
    Dp = np.asarray(L0_attn_D_param, dtype=np.float32)
    tr = np.asarray(L0_attn_trend_emphasis, dtype=np.float32)
    w8 = np.asarray(L0_ffn_w_int8)
    ws = np.asarray(L0_ffn_weight_scale, dtype=np.float32)
    bi = np.asarray(L0_ffn_bias, dtype=np.float32)

    nc = _get_nc()
    in_maps = []
    for c in range(8):
        b, j = c // 2, c % 2
        in_maps.append(_prep_core(x[b], j, A_log, xpw, dtw, dtb, Dp, tr, w8, ws, bi))
    res = run_bass_kernel_spmd(nc, in_maps, core_ids=list(range(8)))
    out = np.empty((4, T, D), dtype=res.results[0]["o"].dtype)
    for c in range(8):
        b, j = c // 2, c % 2
        out[b][:, j * DOWN:(j + 1) * DOWN] = res.results[c]["o"]
    return out
